# version 04
# baseline (speedup 1.0000x reference)
import contextlib
import numpy as np

import concourse.bass as bass
import concourse.mybir as mybir
from concourse.bass_utils import run_bass_kernel_spmd
from concourse.alu_op_type import AluOpType as alu

F32 = mybir.dt.float32
BF16 = mybir.dt.bfloat16
AF = mybir.ActivationFunctionType

NCORES = 8
B = 2
T = 2048
D = 1024
G = 8
GF = 32
CH = 512
NCH = T // CH
LC = 128
EPS = 1e-5

ENGINES = ("sync", "tensor", "vector", "scalar", "gpsimd")

_LIMIT_PHASES = None
_DEBUG_TAPS = ()
_PHASE_SEQ = None
_LAST_RESULTS = None


class Prog:
    def __init__(self, nc, es):
        self.nc = nc
        self.es = es
        self.q = {e: [] for e in ENGINES}
        self.cnt = {}
        self.lastw = {e: {} for e in ENGINES}
        self.sems = {}
        for name in ("sT", "sV", "sA", "sG"):
            self.sems[name] = es.enter_context(nc.semaphore(name))
            self.cnt[name] = 0

    def chan(self, name):
        if name not in self.sems:
            self.sems[name] = self.es.enter_context(self.nc.semaphore(name))
            self.cnt[name] = 0
        return name

    def cur(self, sem):
        return self.cnt[sem]

    def wait(self, eng, sem, val=None):
        if isinstance(sem, tuple):
            sem, val = sem
        if val is None or val <= 0:
            return
        if self.lastw[eng].get(sem, 0) >= val:
            return
        self.lastw[eng][sem] = val
        h = self.sems[sem]
        self.q[eng].append(lambda E, h=h, v=val: E.wait_ge(h, v))

    def barrier(self):
        vals = {s: self.cnt[s] for s in self.sems}
        for eng in ENGINES:
            for s, v in vals.items():
                self.wait(eng, s, v)

    def do(self, eng, fn, sem=None, k=1):
        if sem is None:
            self.q[eng].append(fn)
            return None
        self.cnt[sem] += k
        h = self.sems[sem]
        self.q[eng].append(lambda E, fn=fn, h=h, k=k: fn(E).then_inc(h, k))
        return self.cnt[sem]


class Banks:
    def __init__(self, p, tensors):
        self.p = p
        self.t = tensors
        self.n = len(tensors)
        self.state = [dict(rsem=None, rval=0, wval=0) for _ in tensors]
        self.rr = 0

    def acquire(self):
        i = self.rr % self.n
        self.rr += 1
        s = self.state[i]
        if s["rsem"] is not None:
            self.p.wait("tensor", s["rsem"], s["rval"])
        return i, self.t[i]

    def wrote(self, i):
        self.state[i]["wval"] = self.p.cur("sT")

    def pre_read(self, i, eng):
        self.p.wait(eng, "sT", self.state[i]["wval"])

    def post_read(self, i, sem, val):
        self.state[i]["rsem"] = sem
        self.state[i]["rval"] = val


def build_program(limit_phases=None):
    nc = bass.Bass("TRN2", target_bir_lowering=False)
    es = contextlib.ExitStack()

    dram = {}

    def din(name, shape, dt=F32):
        dram[name] = nc.dram_tensor(name, list(shape), dt, kind="ExternalInput").ap()
        return dram[name]

    def dscratch(name, shape, dt=F32):
        dram[name] = nc.dram_tensor(name, list(shape), dt, kind="Internal").ap()
        return dram[name]

    x_in = din("x", [B, G, 128, T])
    out_d = nc.dram_tensor("out", [B, G, 128, T], F32, kind="ExternalOutput").ap()

    WrT = din("l0aWrT", [D, D]); WkT = din("l0aWkT", [D, D])
    WvT = din("l0aWvT", [D, D]); WoT = din("l0aWoT", [D, D])
    W1T = din("l0aW1T", [D, 4 * D]); W2T = din("l0aW2T", [4 * D, D])
    b1_d = din("l0ab1", [1, 4 * D]); b2_d = din("l0ab2", [1, D])
    ln1g_d = din("l0aln1g", [128, G]); ln1b_d = din("l0aln1b", [128, G])
    ln2g_d = din("l0aln2g", [128, G]); ln2b_d = din("l0aln2b", [128, G])
    fWrT = din("l0fWrT", [D, D]); fWkT = din("l0fWkT", [D, D]); fWvT = din("l0fWvT", [D, D])
    mur_d = din("l0fmur", [128, G]); muk_d = din("l0fmuk", [128, G])
    rkvT = din("l1arkvT", [D, 6 * D]); outwT = din("l1aoutwT", [2 * D, D])
    td_d = din("l1atd", [128, G]); tf_d = din("l1atf", [128, G])
    tdr_d = din("l1atdr", [128, G]); tfr_d = din("l1atfr", [128, G])
    WfcT = din("l1fWfcT", [D, 4 * D]); WprojT = din("l1fWprojT", [4 * D, D])
    lnw_d = din("l1flnw", [128, GF]); lnb_d = din("l1flnb", [128, GF])

    xresA = dscratch("xresA", [B, G, 128, T])
    xresB = dscratch("xresB", [B, G, 128, T])
    rkv_s = dscratch("rkv_s", [B, 3, G, 128, T])
    y_s = dscratch("y_s", [B, G, 128, T], BF16)
    x1_s = dscratch("x1_s", [B, G, 128, T])
    h1_s = dscratch("h1_s", [B, GF, 128, T])
    s1_s = dscratch("s1_s", [B, GF, 128, T], BF16)
    h2_s = dscratch("h2_s", [B, G, 128, T])
    ffn_s = dscratch("ffn_s", [B, G, 128, T], BF16)
    outb_s = dscratch("outb_s", [B, G, 128, T], BF16)
    wfcq_s = dscratch("wfcq_s", [D, 4 * D], BF16)
    wprq_s = dscratch("wprq_s", [4 * D, D], BF16)
    qrows = dscratch("qrows", [1, 7680])
    qthcol = dscratch("qthcol", [128, 4])
    qlifst = dscratch("qlifst", [128, 64])

    def sb(name, shape, dt=F32):
        return es.enter_context(nc.sbuf_tensor(name, list(shape), dt))

    ones1b = sb("ones1b", [128, 1], BF16)
    ones1f = sb("ones1f", [1, 128], F32)
    onescolf = sb("onescolf", [128, 1], F32)
    eps11 = sb("eps11", [1, 1], F32)
    vec = {}
    for nm in ("mur", "muk", "mur1", "muk1", "ln1g", "ln1b", "ln2g", "ln2b",
               "wf", "uf", "wr", "ur"):
        vec[nm] = sb("v_" + nm, [128, G], F32)
    lnw_sb = sb("v_lnw", [128, GF], F32)
    lnb_sb = sb("v_lnb", [128, GF], BF16)
    rowar = sb("rowar", [1, 7680], F32)
    b1h = rowar[0:1, 0:4 * D]
    b2h = rowar[0:1, 4 * D:5 * D]
    onesrow = rowar[0:1, 5 * D:5 * D + CH]
    ws1neg = rowar[0:1, 5 * D + CH:5 * D + CH + D]
    throw = rowar[0:1, 5 * D + CH + D:5 * D + CH + D + 8]
    srow = [sb(f"srow{i}", [1, CH], F32) for i in range(6)]

    wbig = sb("wbig", [128, 3 * G * D], BF16)

    F = [sb(f"F{i}", [128, G * CH], F32) for i in range(3)]
    Bf = [sb(f"Bf{i}", [128, G * CH], BF16) for i in range(4)]
    xn = [sb(f"xn{b}", [128, G * (CH + 1)], BF16) for b in range(B)]
    tpbuf = sb("tpbuf", [128, 2 * CH], F32)
    tp = [tpbuf[:, b * CH:(b + 1) * CH] for b in range(B)]
    gtbuf = sb("gtbuf", [128, 4 * CH], F32)
    gt = [gtbuf[:, i * CH:(i + 1) * CH] for i in range(4)]
    bcbuf = sb("bcbuf", [128, 4 * CH], F32)
    rbc = [bcbuf[:, b * CH:(b + 1) * CH] for b in range(B)]
    rmubc = [bcbuf[:, (2 + b) * CH:(3 + b) * CH] for b in range(B)]

    c09 = sb("c09", [128, 1], F32)
    cbh = sb("cbh", [128, G], F32)
    cbacc = sb("cbacc", [128, G], F32)
    thcol = sb("thcol", [128, 4], F32)

    lifst = sb("lifst", [128, 64], F32)
    ybuf = gtbuf.bitcast(BF16)
    kvb = tpbuf.bitcast(BF16)

    xch = [F[0], F[1]]
    kact_ar = F[2].bitcast(BF16)
    kact = [kact_ar[:, b * G * CH:(b + 1) * G * CH] for b in range(B)]
    xbf = [Bf[0], Bf[1]]
    xsq = [Bf[2], Bf[3]]
    mixr = [Bf[0], Bf[1]]
    mixk = [Bf[2], Bf[3]]

    ps = [es.enter_context(nc.psum_tensor(f"ps{i}", [128, CH], F32)) for i in range(6)]
    stps = [es.enter_context(nc.psum_tensor(f"stps{i}", [1, CH], F32)) for i in range(2)]

    p = Prog(nc, es)
    banks = Banks(p, ps)
    stbanks = Banks(p, stps)

    def dma_load(dst_ap, src_ap, ch):
        c = p.chan("L" + ch)
        v = p.do("sync", lambda E, d=dst_ap, s=src_ap: E.dma_start(out=d, in_=s), c, 16)
        return (c, v)

    def dma_store(dst_ap, src_ap, ch):
        c = p.chan("S" + ch)
        v = p.do("sync", lambda E, d=dst_ap, s=src_ap: E.dma_start(out=d, in_=s), c, 16)
        return (c, v)

    def dma_cast(dst_ap, src_ap, ch="w"):
        c = p.chan("W" + ch)
        v = p.do("gpsimd", lambda E, d=dst_ap, s=src_ap: E.dma_start(out=d, in_=s), c, 16)
        return (c, v)

    def xslice(dram_t, b, c0, w=CH):
        return dram_t[b, :, :, c0:c0 + w].transpose([1, 0, 2])

    def xslice3(dram_t, b, j, c0, w=CH):
        return dram_t[b, j, :, :, c0:c0 + w].transpose([1, 0, 2])

    def load_weight(dst_off, src, din_, dout, dst_buf=None):
        dbuf = dst_buf if dst_buf is not None else wbig
        nkt = din_ // 128
        ap = src.rearrange("(k p) o -> k p o", p=128).transpose([1, 0, 2])
        return dma_cast(dbuf[:, dst_off:dst_off + nkt * dout], ap)

    def cast_and_square(src, load_vals, also_sq=True):
        cvals, svals = [], []
        for b in range(B):
            p.wait("scalar", load_vals[b])
            cv = p.do("scalar", lambda E, o=xbf[b][:], a=src[b][:]:
                      E.activation(out=o, in_=a, func=AF.Copy, bias=0.0, scale=1.0), "sA")
            cvals.append(cv)
        if also_sq:
            for b in range(B):
                sv = p.do("scalar", lambda E, o=xsq[b][:], a=src[b][:]:
                          E.activation(out=o, in_=a, func=AF.Square, bias=0.0, scale=1.0), "sA")
                svals.append(sv)
        return cvals, svals

    def ln_evict(handles, dsts, inv, width):
        for b in range(B):
            trk, i, bank = handles[b]
            trk.pre_read(i, "vector")
            v = p.do("vector", lambda E, o=dsts[b], s=bank, iv=inv, w=width:
                     E.tensor_scalar(out=o, in0=s[0:1, 0:w], scalar1=iv, scalar2=None,
                                     op0=alu.mult), "sV")
            trk.post_read(i, "sV", v)

    def ln_finish(sums, sqs, inv, width=CH, trk=None):
        mus = [srow[b][0:1, 0:width] for b in range(B)]
        tmps = [srow[2 + b][0:1, 0:width] for b in range(B)]
        ln_evict(sums, mus, inv, width)
        ln_evict(sqs, tmps, inv, width)
        return ln_rest(width, trk if trk is not None else banks)

    def ln_rest(width, trk):
        mus = [srow[b][0:1, 0:width] for b in range(B)]
        tmps = [srow[2 + b][0:1, 0:width] for b in range(B)]
        rs = [srow[4 + b][0:1, 0:width] for b in range(B)]
        for b in range(B):
            p.do("vector", lambda E, o=rs[b], a=mus[b]:
                 E.tensor_tensor(out=o, in0=a, in1=a, op=alu.mult), "sV")
        for b in range(B):
            p.do("vector", lambda E, o=tmps[b], a=tmps[b], bb=rs[b]:
                 E.tensor_tensor(out=o, in0=a, in1=bb, op=alu.subtract), "sV")
        p.wait("scalar", "sV", p.cur("sV"))
        for b in range(B):
            p.do("scalar", lambda E, o=tmps[b], a=tmps[b]:
                 E.activation(out=o, in_=a, func=AF.Ln, bias=eps11[:], scale=1.0), None)
        for b in range(B):
            p.do("scalar", lambda E, o=rs[b], a=tmps[b]:
                 E.activation(out=o, in_=a, func=AF.Exp, bias=0.0, scale=-0.5), "sA")
        av = p.cur("sA")
        p.wait("vector", "sA", av)
        for b in range(B):
            p.do("vector", lambda E, o=tmps[b], a=rs[b], bb=mus[b]:
                 E.tensor_tensor(out=o, in0=a, in1=bb, op=alu.mult), "sV")
        rmuv = p.cur("sV")
        p.wait("tensor", "sA", av)
        p.wait("tensor", "sV", rmuv)
        p.wait("scalar", "sV", p.cur("sV"))
        for b in range(B):
            for row, dst in ((rs[b], rbc[b]), (tmps[b], rmubc[b])):
                i, bank = trk.acquire()
                p.do("tensor", lambda E, bk=bank, w=ones1f, r=row, wd=width:
                     E.matmul(bk[:, 0:wd], w[:], r, start=True, stop=True), "sT")
                trk.wrote(i)
                trk.pre_read(i, "scalar")
                v = p.do("scalar", lambda E, o=dst, s=bank, wd=width:
                         E.activation(out=o[:, 0:wd], in_=s[:, 0:wd], func=AF.Copy,
                                      bias=0.0, scale=1.0), "sA")
                trk.post_read(i, "sA", v)
        return p.cur("sA")

    def ln_stats(src_bf, src_sq, ntile, cast_vals, sq_vals, trk=None):
        inv = 1.0 / (ntile * 128)
        mus = [srow[b][0:1, 0:CH] for b in range(B)]
        tmps = [srow[2 + b][0:1, 0:CH] for b in range(B)]
        for key, (src, vals, dsts) in (("s", (src_bf, cast_vals, mus)),
                                       ("q", (src_sq, sq_vals, tmps))):
            hl = []
            for b in range(B):
                i, bank = stbanks.acquire()
                p.wait("tensor", "sA", vals[b])
                for g in range(ntile):
                    p.do("tensor", lambda E, bk=bank, w=ones1b,
                         r=src[b][:, g * CH:(g + 1) * CH],
                         st=(g == 0), sp=(g == ntile - 1):
                         E.matmul(bk[:], w[:], r, start=st, stop=sp),
                         "sT" if g == ntile - 1 else None)
                stbanks.wrote(i)
                hl.append((stbanks, i, bank))
            ln_evict(hl, dsts, inv, CH)
        return ln_rest(CH, trk if trk is not None else banks)

    def build_xn(src, bcast_val, dst=None):
        dstb = dst if dst is not None else xn
        p.wait("vector", "sA", bcast_val)
        for g in range(G):
            for b in range(B):
                p.do("vector", lambda E, o=tp[b][:],
                     a=src[b][:, g * CH:(g + 1) * CH], r_=rbc[b][:]:
                     E.tensor_tensor(out=o, in0=a, in1=r_, op=alu.mult), "sV")
            for b in range(B):
                p.do("vector", lambda E, o=dstb[b][:, g * (CH + 1) + 1:(g + 1) * (CH + 1)],
                     a=tp[b][:], m=rmubc[b][:]:
                     E.tensor_tensor(out=o, in0=a, in1=m, op=alu.subtract), "sV")
        return p.cur("sV")

    def mm_group(bank_ap, lhs_slices, rhs_slices, extra=None):
        n = len(lhs_slices)
        for g in range(n):
            last = (g == n - 1) and extra is None
            p.do("tensor", lambda E, bk=bank_ap, w=lhs_slices[g], r=rhs_slices[g],
                 st=(g == 0), sp=last:
                 E.matmul(bk[:], w, r, start=st, stop=sp),
                 "sT" if last else None)
        if extra is not None:
            wx, rx = extra
            p.do("tensor", lambda E, bk=bank_ap, w=wx, r=rx:
                 E.matmul(bk[:], w, r, start=False, stop=True), "sT")
        return p.cur("sT")

    def wslices(off, m, nkt, dout, buf=None):
        bf = buf if buf is not None else wbig
        return [bf[:, off + g * dout + m * 128: off + g * dout + m * 128 + 128]
                for g in range(nkt)]

    def rslices(buf_b, nkt):
        return [buf_b[:, g * CH:(g + 1) * CH] for g in range(nkt)]

    def phase0():
        p.do("gpsimd", lambda E: E.memset(ones1b[:], 1.0), None)
        p.do("gpsimd", lambda E: E.memset(ones1f[:], 1.0), None)
        p.do("gpsimd", lambda E: E.memset(onesrow[:], 1.0), None)
        p.do("gpsimd", lambda E: E.memset(onescolf[:], 1.0), None)
        p.do("gpsimd", lambda E: E.memset(eps11[:], EPS), "sG")
        for nm, d_ in (("mur", mur_d), ("muk", muk_d), ("ln1g", ln1g_d), ("ln1b", ln1b_d),
                       ("ln2g", ln2g_d), ("ln2b", ln2b_d), ("uf", tf_d), ("ur", tfr_d),
                       ("wf", td_d), ("wr", tdr_d)):
            dma_load(vec[nm][:], d_[:], "sm")
        dma_load(lnw_sb[:], lnw_d[:], "sm")
        dma_load(b1h[:], b1_d[:], "sm")
        ld_all = dma_load(b2h[:], b2_d[:], "sm")
        dma_cast(lnb_sb[:], lnb_d[:])
        p.wait("vector", ld_all)
        for nm, src_ in (("mur1", "mur"), ("muk1", "muk")):
            p.do("vector", lambda E, o=vec[nm][:], a=vec[src_][:]:
                 E.tensor_scalar(out=o, in0=a, scalar1=-1.0, scalar2=1.0,
                                 op0=alu.mult, op1=alu.add), "sV")
        p.wait("scalar", ld_all)
        for nm in ("wf", "wr"):
            p.do("scalar", lambda E, o=vec[nm][:], a=vec[nm][:]:
                 E.activation(out=o, in_=a, func=AF.Exp, bias=0.0, scale=1.0), "sA")
        sv = p.cur("sA")
        p.wait("vector", "sA", sv)
        for nm in ("wf", "wr"):
            p.do("vector", lambda E, o=vec[nm][:], a=vec[nm][:]:
                 E.tensor_scalar(out=o, in0=a, scalar1=-1.0, scalar2=None, op0=alu.mult), "sV")

    def phase_l0ffn(src_dram, dst_dram):
        wv = [load_weight(0, fWrT, D, D), load_weight(G * D, fWkT, D, D),
              load_weight(2 * G * D, fWvT, D, D)]
        p.do("gpsimd", lambda E, o=xn[0][:, 0::CH + 1]: E.memset(o, 0.0), None)
        p.do("gpsimd", lambda E, o=xn[1][:, 0::CH + 1]: E.memset(o, 0.0), "sG")
        guard_ready = p.cur("sG")
        prev_ov = p.cur("sV")
        prev_st = []
        prev_mm = p.cur("sT")
        for c in range(NCH):
            c0 = c * CH
            p.wait("sync", "sV", prev_ov)
            for tok in prev_st:
                p.wait("sync", tok)
            lv = [dma_load(xch[b][:], xslice(src_dram, b, c0), f"x{b}") for b in range(B)]
            p.wait("scalar", "sT", prev_mm)
            cvals, svals = cast_and_square(xch, lv)
            bval = ln_stats(xbf, xsq, G, cvals, svals)
            if c > 0:
                for b in range(B):
                    p.do("vector", lambda E, o=xn[b][:, 0::CH + 1], a=xn[b][:, CH::CH + 1]:
                         E.tensor_copy(out=o, in_=a), "sV")
            else:
                p.wait("vector", "sG", guard_ready)
            build_xn(xch, bval)
            p.wait("vector", "sT", p.cur("sT"))
            for g in range(G):
                for b in range(B):
                    p.do("vector", lambda E, o=tp[b][:],
                         a=xn[b][:, g * (CH + 1):g * (CH + 1) + CH], m1=vec["mur1"][:, g:g + 1]:
                         E.tensor_scalar(out=o, in0=a, scalar1=m1, scalar2=None, op0=alu.mult), "sV")
                for b in range(B):
                    p.do("vector", lambda E, o=mixr[b][:, g * CH:(g + 1) * CH],
                         a=xn[b][:, g * (CH + 1) + 1:(g + 1) * (CH + 1)],
                         m=vec["mur"][:, g:g + 1], t=tp[b][:]:
                         E.scalar_tensor_tensor(out=o, in0=a, scalar=m, in1=t,
                                                op0=alu.mult, op1=alu.add), "sV")
            for g in range(G):
                for b in range(B):
                    p.do("vector", lambda E, o=tp[b][:],
                         a=xn[b][:, g * (CH + 1):g * (CH + 1) + CH], m1=vec["muk1"][:, g:g + 1]:
                         E.tensor_scalar(out=o, in0=a, scalar1=m1, scalar2=None, op0=alu.mult), "sV")
                for b in range(B):
                    p.do("vector", lambda E, o=mixk[b][:, g * CH:(g + 1) * CH],
                         a=xn[b][:, g * (CH + 1) + 1:(g + 1) * (CH + 1)],
                         m=vec["muk"][:, g:g + 1], t=tp[b][:]:
                         E.scalar_tensor_tensor(out=o, in0=a, scalar=m, in1=t,
                                                op0=alu.mult, op1=alu.add), "sV")
            mixv = p.cur("sV")
            p.wait("tensor", wv[2])
            p.wait("tensor", "sV", mixv)
            p.wait("scalar", "sV", prev_ov)
            for m in range(G):
                relus = []
                for b in range(B):
                    i, bank = banks.acquire()
                    mm_group(bank, wslices(G * D, m, G, D), rslices(mixk[b], G))
                    banks.wrote(i)
                    banks.pre_read(i, "scalar")
                    tslot = gt[(m % 2) * 2 + b]
                    v = p.do("scalar", lambda E, o=tslot[:], s=bank:
                             E.activation(out=o, in_=s[:], func=AF.Relu, bias=0.0,
                                          scale=1.0), "sA")
                    banks.post_read(i, "sA", v)
                    relus.append(tslot)
                for b in range(B):
                    p.do("scalar", lambda E, o=kact[b][:, m * CH:(m + 1) * CH], a=relus[b][:]:
                         E.activation(out=o, in_=a, func=AF.Square, bias=0.0, scale=1.0), "sA")
            kvv = p.cur("sA")
            p.wait("tensor", "sA", kvv)
            for m in range(G):
                gates = []
                for b in range(B):
                    i, bank = banks.acquire()
                    mm_group(bank, wslices(0, m, G, D), rslices(mixr[b], G))
                    banks.wrote(i)
                    banks.pre_read(i, "scalar")
                    gslot = gt[(m % 2) * 2 + b]
                    v = p.do("scalar", lambda E, o=gslot[:], s=bank:
                             E.activation(out=o, in_=s[:], func=AF.Sigmoid, bias=0.0,
                                          scale=1.0), "sA")
                    banks.post_read(i, "sA", v)
                    gates.append((gslot, v))
                hb = []
                for b in range(B):
                    i, bank = banks.acquire()
                    mm_group(bank, wslices(2 * G * D, m, G, D), rslices(kact[b], G))
                    banks.wrote(i)
                    hb.append((i, bank))
                for b in range(B):
                    i, bank = hb[b]
                    banks.pre_read(i, "vector")
                    p.wait("vector", "sA", gates[b][1])
                    p.do("vector", lambda E, o=tp[b][:], s=bank, r_=gates[b][0][:]:
                         E.tensor_tensor(out=o, in0=s[:], in1=r_, op=alu.mult), None)
                for b in range(B):
                    i, bank = hb[b]
                    v = p.do("vector", lambda E, o=xch[b][:, m * CH:(m + 1) * CH],
                             t=tp[b][:], xx=xch[b][:, m * CH:(m + 1) * CH]:
                             E.scalar_tensor_tensor(out=o, in0=t, scalar=0.5, in1=xx,
                                                    op0=alu.mult, op1=alu.add), "sV")
                    banks.post_read(i, "sV", v)
            ov = p.cur("sV")
            p.wait("sync", "sV", ov)
            prev_st = [dma_store(xslice(dst_dram, b, c0), xch[b][:], f"ox{b}")
                       for b in range(B)]
            prev_ov = ov
            prev_mm = p.cur("sT")

    def phase_l0attn_rkv(src_dram):
        wv = [load_weight(0, WrT, D, D), load_weight(G * D, WkT, D, D),
              load_weight(2 * G * D, WvT, D, D)]
        prev_xv = p.cur("sV")
        prev_mm = p.cur("sT")
        slot_st = [None] * 4
        for c in range(NCH):
            c0 = c * CH
            p.wait("sync", "sV", prev_xv)
            lv = [dma_load(xch[b][:], xslice(src_dram, b, c0), f"x{b}") for b in range(B)]
            p.wait("scalar", "sT", prev_mm)
            cvals, svals = cast_and_square(xch, lv)
            bval = ln_stats(xbf, xsq, G, cvals, svals)
            xnv = build_xn(xch, bval)
            prev_xv = xnv
            p.wait("tensor", "sV", xnv)
            p.wait("tensor", wv[2])
            xnsl = [[xn[b][:, g * (CH + 1) + 1:(g + 1) * (CH + 1)] for g in range(G)]
                    for b in range(B)]
            si = 0
            for j in range(3):
                for m in range(G):
                    for b in range(B):
                        i, bank = banks.acquire()
                        mm_group(bank, wslices(j * G * D, m, G, D), xnsl[b])
                        banks.wrote(i)
                        banks.pre_read(i, "scalar")
                        slot = si % 4
                        si += 1
                        if slot_st[slot]:
                            p.wait("scalar", slot_st[slot])
                        v = p.do("scalar", lambda E, o=gt[slot][:], s=bank:
                                 E.activation(out=o, in_=s[:], func=AF.Copy, bias=0.0,
                                              scale=0.5), "sA")
                        banks.post_read(i, "sA", v)
                        p.wait("sync", "sA", v)
                        slot_st[slot] = dma_store(rkv_s[b, j, m, :, c0:c0 + CH],
                                                  gt[slot][:], f"st{slot}")
            prev_mm = p.cur("sT")

    def phase_mixer():
        NLC = T // LC
        p.do("gpsimd", lambda E: E.memset(lifst[:], 0.0), None)
        p.do("gpsimd", lambda E: E.memset(c09[:], 0.9), "sG")
        p.wait("vector", "sG", p.cur("sG"))
        wbigf = wbig.bitcast(F32)
        rin = [[F[0][:, 0:24 * LC], F[1][:, 0:24 * LC]],
               [wbigf[:, 0:24 * LC], wbigf[:, 24 * LC:48 * LC]]]
        vprA = F[2][:, 0:24 * LC]
        vprB = wbigf[:, 6144:6144 + 24 * LC]
        spk = [Bf[0][:, 0:24 * LC], Bf[1][:, 0:24 * LC]]
        hH = [xn[0].bitcast(F32)[:, 0:2 * 8 * LC], xn[1].bitcast(F32)[:, 0:2 * 8 * LC]]
        prev_sg = p.cur("sG")
        prev_sv = p.cur("sV")
        ydma = {0: None, 1: None}
        for lc in range(NLC):
            t0 = lc * LC
            ring = lc % 2
            p.wait("sync", "sV", prev_sv if lc >= 2 else 0)
            lvs = []
            for b in range(B):
                lvs.append(dma_load(rin[ring][b][:],
                                    rkv_s[b, :, :, :, t0:t0 + LC].transpose([2, 0, 1, 3]),
                                    f"r{ring}{b}"))
            p.wait("vector", lvs[0])
            p.wait("vector", lvs[1])
            p.wait("vector", "sG", prev_sg)
            for t in range(LC):
                p.do("vector", lambda E, o=vprA[:, t::LC], s=lifst[:, 0:24],
                     xx=rin[ring][0][:, t::LC]:
                     E.scalar_tensor_tensor(out=o, in0=s, scalar=0.5, in1=xx,
                                            op0=alu.mult, op1=alu.add), None)
                p.do("vector", lambda E, o=vprB[:, t::LC], s=lifst[:, 24:48],
                     xx=rin[ring][1][:, t::LC]:
                     E.scalar_tensor_tensor(out=o, in0=s, scalar=0.5, in1=xx,
                                            op0=alu.mult, op1=alu.add), None)
                p.do("vector", lambda E, o=lifst[:, 0:24], a=vprA[:, t::LC]:
                     E.scalar_tensor_tensor(out=o, in0=a, scalar=1.0, in1=a,
                                            op0=alu.is_lt, op1=alu.mult), None)
                p.do("vector", lambda E, o=lifst[:, 24:48], a=vprB[:, t::LC]:
                     E.scalar_tensor_tensor(out=o, in0=a, scalar=1.0, in1=a,
                                            op0=alu.is_lt, op1=alu.mult), None)
            lifv = p.do("vector", lambda E: E.engine_nop(), "sV")
            p.wait("gpsimd", "sV", lifv)
            for b in range(B):
                vp = vprA if b == 0 else vprB
                for j in range(3):
                    p.do("gpsimd", lambda E, o=spk[b][:, j * 8 * LC:(j + 1) * 8 * LC],
                         a=vp[:, j * 8 * LC:(j + 1) * 8 * LC]:
                         E.tensor_single_scalar(out=o, in_=a, scalar=1.0, op=alu.is_ge), None)
            for b in range(B):
                p.do("gpsimd", lambda E, o=kvb[:, b * 8 * LC:(b + 1) * 8 * LC],
                     a=spk[b][:, 8 * LC:16 * LC], bb=spk[b][:, 16 * LC:24 * LC]:
                     E.tensor_tensor(out=o, in0=a, in1=bb, op=alu.mult), None)
            kvg = p.do("gpsimd", lambda E: E.engine_nop(), "sG")
            prev_sg = kvg
            p.wait("vector", "sG", kvg)
            hv0 = p.cur("sV")
            for g in range(G):
                for b in range(B):
                    hr = hH[b][:, (ring * 8 + g) * LC:(ring * 8 + g + 1) * LC]
                    if lc == 0:
                        ini = 0.0
                    else:
                        pr = hH[b][:, ((1 - ring) * 8 + g) * LC + LC - 1:
                                   ((1 - ring) * 8 + g) * LC + LC]
                        ini = pr
                    p.do("vector", lambda E, o=hr, d0=c09[:].broadcast_to([128, LC]),
                         d1=kvb[:, (b * 8 + g) * LC:(b * 8 + g + 1) * LC], i_=ini:
                         E.tensor_tensor_scan(out=o, data0=d0, data1=d1, initial=i_,
                                              op0=alu.mult, op1=alu.add), None)
            hv = p.do("vector", lambda E: E.engine_nop(), "sV")
            prev_sv = hv
            p.wait("gpsimd", "sV", hv)
            if ydma[ring]:
                p.wait("gpsimd", ydma[ring])
            for b in range(B):
                p.do("gpsimd", lambda E, o=ybuf[:, (ring * 16 + b * 8) * LC:
                                                (ring * 16 + b * 8 + 8) * LC],
                     a=spk[b][:, 0:8 * LC],
                     h_=hH[b][:, ring * 8 * LC:(ring + 1) * 8 * LC]:
                     E.tensor_tensor(out=o, in0=a, in1=h_, op=alu.mult), None)
            yg = p.do("gpsimd", lambda E: E.engine_nop(), "sG")
            p.wait("sync", "sG", yg)
            ydma[ring] = dma_store(y_s[:, :, :, t0:t0 + LC].transpose([2, 0, 1, 3]),
                                   ybuf[:, ring * 16 * LC:(ring + 1) * 16 * LC],
                                   f"y{ring}")

    def phase_x1(src_dram):
        wv = load_weight(0, WoT, D, D)
        prev_mm = p.cur("sT")
        prev_sv = p.cur("sV")
        prev_st = []
        ybig = [Bf[0], Bf[1]]
        xbf2 = [Bf[2], Bf[3]]
        xsq2 = [kact_ar[:, 0:G * CH], kact_ar[:, G * CH:2 * G * CH]]
        for c in range(NCH):
            c0 = c * CH
            p.wait("sync", "sV", prev_sv)
            for tok in prev_st:
                p.wait("sync", tok)
            lv = [dma_load(xch[b][:], xslice(src_dram, b, c0), f"x{b}") for b in range(B)]
            lvy = [dma_load(ybig[b][:], xslice(y_s, b, c0), f"a{b}") for b in range(B)]
            cvals, svals = [], []
            for b in range(B):
                p.wait("scalar", lv[b])
                p.wait("scalar", "sT", prev_mm)
                cv = p.do("scalar", lambda E, o=xbf2[b][:], a=xch[b][:]:
                          E.activation(out=o, in_=a, func=AF.Copy, bias=0.0, scale=1.0), "sA")
                cvals.append(cv)
            for b in range(B):
                sv = p.do("scalar", lambda E, o=xsq2[b][:], a=xch[b][:]:
                          E.activation(out=o, in_=a, func=AF.Square, bias=0.0, scale=1.0), "sA")
                svals.append(sv)
            bval = ln_stats(xbf2, xsq2, G, cvals, svals)
            p.wait("tensor", wv)
            for tok in lvy:
                p.wait("tensor", tok)
            p.wait("vector", "sA", bval)
            for m in range(G):
                hb = []
                for b in range(B):
                    i, bank = banks.acquire()
                    mm_group(bank, wslices(0, m, G, D), rslices(ybig[b], G))
                    banks.wrote(i)
                    hb.append((i, bank))
                for b in range(B):
                    p.do("vector", lambda E, o=tp[b][:],
                         a=xch[b][:, m * CH:(m + 1) * CH], r_=rbc[b][:]:
                         E.tensor_tensor(out=o, in0=a, in1=r_, op=alu.mult), None)
                for b in range(B):
                    p.do("vector", lambda E, o=tp[b][:], a=tp[b][:], m_=rmubc[b][:]:
                         E.tensor_tensor(out=o, in0=a, in1=m_, op=alu.subtract), None)
                for b in range(B):
                    i, bank = hb[b]
                    banks.pre_read(i, "vector")
                    v = p.do("vector", lambda E, o=xch[b][:, m * CH:(m + 1) * CH],
                             s=bank, t=tp[b][:]:
                             E.tensor_tensor(out=o, in0=s[:], in1=t, op=alu.add), "sV")
                    banks.post_read(i, "sV", v)
            x1pv = p.cur("sV")
            cv2, sv2 = [], []
            p.wait("scalar", "sV", x1pv)
            p.wait("scalar", "sT", p.cur("sT"))
            for b in range(B):
                cv = p.do("scalar", lambda E, o=xbf2[b][:], a=xch[b][:]:
                          E.activation(out=o, in_=a, func=AF.Copy, bias=0.0, scale=1.0), "sA")
                cv2.append(cv)
            for b in range(B):
                sv = p.do("scalar", lambda E, o=xsq2[b][:], a=xch[b][:]:
                          E.activation(out=o, in_=a, func=AF.Square, bias=0.0, scale=1.0), "sA")
                sv2.append(sv)
            bval2 = ln_stats(xbf2, xsq2, G, cv2, sv2)
            p.wait("vector", "sA", bval2)
            for m in range(G):
                for b in range(B):
                    p.do("vector", lambda E, o=tp[b][:],
                         a=xch[b][:, m * CH:(m + 1) * CH], r_=rbc[b][:]:
                         E.tensor_tensor(out=o, in0=a, in1=r_, op=alu.mult), None)
                for b in range(B):
                    p.do("vector", lambda E, o=gt[b][:], a=tp[b][:], m_=rmubc[b][:]:
                         E.tensor_tensor(out=o, in0=a, in1=m_, op=alu.subtract), None)
                for b in range(B):
                    v = p.do("vector", lambda E, o=xch[b][:, m * CH:(m + 1) * CH],
                             a=gt[b][:], g1=vec["ln1g"][:, m:m + 1], bb1=vec["ln1b"][:, m:m + 1]:
                             E.tensor_scalar(out=o, in0=a, scalar1=g1, scalar2=bb1,
                                             op0=alu.mult, op1=alu.add), "sV")
            ov = p.cur("sV")
            p.wait("sync", "sV", ov)
            prev_st = [dma_store(xslice(x1_s, b, c0), xch[b][:], f"ox{b}")
                       for b in range(B)]
            prev_mm = p.cur("sT")
            prev_sv = ov

    def phase_w1():
        prev_mm = p.cur("sT")
        slot_st = [None] * 4
        slot_mm = {}
        for c in range(NCH):
            c0 = c * CH
            p.wait("sync", "sA", p.cur("sA") if c else 0)
            lv = [dma_load(xch[b][:], xslice(x1_s, b, c0), f"x{b}") for b in range(B)]
            cvals = []
            for b in range(B):
                p.wait("scalar", lv[b])
                p.wait("scalar", "sT", prev_mm)
                cv = p.do("scalar", lambda E, o=xbf[b][:], a=xch[b][:]:
                          E.activation(out=o, in_=a, func=AF.Copy, bias=0.0, scale=1.0), "sA")
                cvals.append(cv)
            for mg in range(4):
                slot = mg % 2
                p.wait("gpsimd", "sT", slot_mm.get(slot, 0))
                ap = W1T.rearrange("(k p) o -> k p o", p=128)[:, :, mg * D:(mg + 1) * D]
                ap = ap.transpose([1, 0, 2])
                pv = dma_cast(wbig[:, slot * G * D:slot * G * D + G * D], ap, f"w{slot}")
                p.wait("tensor", pv)
                p.wait("tensor", "sA", max(cvals))
                for m in range(G):
                    mglob = mg * G + m
                    for b in range(B):
                        i, bank = banks.acquire()
                        mm_group(bank, wslices(slot * G * D, m, G, D),
                                 rslices(xbf[b], G),
                                 extra=(b1h[0:1, mglob * 128:(mglob + 1) * 128],
                                        onesrow[0:1, 0:CH]))
                        banks.wrote(i)
                        banks.pre_read(i, "scalar")
                        st_ = (mglob * B + b) % 4
                        if slot_st[st_]:
                            p.wait("scalar", slot_st[st_])
                        v = p.do("scalar", lambda E, o=gt[st_][:], s=bank:
                                 E.activation(out=o, in_=s[:], func=AF.Copy, bias=0.0,
                                              scale=0.5), "sA")
                        banks.post_read(i, "sA", v)
                        p.wait("sync", "sA", v)
                        slot_st[st_] = dma_store(h1_s[b, mglob, :, c0:c0 + CH],
                                                 gt[st_][:], f"st{st_}")
                slot_mm[slot] = p.cur("sT")
            prev_mm = p.cur("sT")

    def phase_lif(src_dram, dst_dram, ngrp, lc):
        nlc = T // lc
        W = ngrp
        p.do("gpsimd", lambda E: E.memset(lifst[:], 0.0), "sG")
        p.wait("vector", "sG", p.cur("sG"))
        nf32 = W * lc
        wbigf = wbig.bitcast(F32)
        if ngrp == 32 and lc == 64:
            rin = [[F[0][:, 0:nf32], F[1][:, 0:nf32]],
                   [F[2][:, 0:nf32], wbigf[:, 0:nf32]]]
            vpr = [wbigf[:, 4096:4096 + nf32], wbigf[:, 4096 + nf32:4096 + 2 * nf32]]
        else:
            rin = [[F[0][:, 0:nf32], F[1][:, 0:nf32]],
                   [F[2][:, 0:nf32], wbigf[:, 0:nf32]]]
            vpr = [wbigf[:, 4096:4096 + nf32], wbigf[:, 4096 + nf32:4096 + 2 * nf32]]
        spk = [Bf[0].bitcast(BF16)[:, 0:nf32], Bf[1][:, 0:nf32]]
        prev_sg = 0
        sdma = {0: None, 1: None}
        for lcx in range(nlc):
            t0 = lcx * lc
            ring = lcx % 2
            lvs = []
            p.wait("sync", "sV", p.cur("sV") if lcx >= 2 else 0)
            for b in range(B):
                lvs.append(dma_load(rin[ring][b][:],
                                    src_dram[b, :, :, t0:t0 + lc].transpose([1, 0, 2]),
                                    f"r{ring}{b}"))
            p.wait("vector", lvs[0])
            p.wait("vector", lvs[1])
            p.wait("vector", "sG", prev_sg)
            for t in range(lc):
                p.do("vector", lambda E, o=vpr[0][:, t::lc], s=lifst[:, 0:W],
                     xx=rin[ring][0][:, t::lc]:
                     E.scalar_tensor_tensor(out=o, in0=s, scalar=0.5, in1=xx,
                                            op0=alu.mult, op1=alu.add), None)
                p.do("vector", lambda E, o=vpr[1][:, t::lc], s=lifst[:, W:2 * W],
                     xx=rin[ring][1][:, t::lc]:
                     E.scalar_tensor_tensor(out=o, in0=s, scalar=0.5, in1=xx,
                                            op0=alu.mult, op1=alu.add), None)
                p.do("vector", lambda E, o=lifst[:, 0:W], a=vpr[0][:, t::lc]:
                     E.scalar_tensor_tensor(out=o, in0=a, scalar=1.0, in1=a,
                                            op0=alu.is_lt, op1=alu.mult), None)
                p.do("vector", lambda E, o=lifst[:, W:2 * W], a=vpr[1][:, t::lc]:
                     E.scalar_tensor_tensor(out=o, in0=a, scalar=1.0, in1=a,
                                            op0=alu.is_lt, op1=alu.mult), None)
            lifv = p.do("vector", lambda E: E.engine_nop(), "sV")
            p.wait("gpsimd", "sV", lifv)
            if sdma[ring]:
                p.wait("gpsimd", sdma[ring])
            for b in range(B):
                p.do("gpsimd", lambda E, o=spk[b][:], a=vpr[b][:]:
                     E.tensor_single_scalar(out=o, in_=a, scalar=1.0, op=alu.is_ge), None)
            sg = p.do("gpsimd", lambda E: E.engine_nop(), "sG")
            prev_sg = sg
            p.wait("sync", "sG", sg)
            for b in range(B):
                sdma[ring] = dma_store(dst_dram[b, :, :, t0:t0 + lc].transpose([1, 0, 2]),
                                       spk[b][:], f"y{ring}")

    def phase_w2():
        CW = 256
        wbigf = wbig
        w2a = load_weight(0, W2T[0:24 * 128, :], 24 * 128, D)
        f0b = F[0].bitcast(BF16)
        f1b = F[1].bitcast(BF16)
        ap = W2T[24 * 128:, :].rearrange("(k p) o -> k p o", p=128).transpose([1, 0, 2])
        dma_cast(f0b[:, 0:4 * D], ap[:, 0:4, :])
        w2c = dma_cast(f1b[:, 0:4 * D], ap[:, 4:8, :])

        def w2slices(m):
            res = [wbig[:, g * D + m * 128:g * D + m * 128 + 128] for g in range(24)]
            res += [f0b[:, g * D + m * 128:g * D + m * 128 + 128] for g in range(4)]
            res += [f1b[:, g * D + m * 128:g * D + m * 128 + 128] for g in range(4)]
            return res

        def s1sl(b, g):
            buf = Bf[2 * b + (g // 16)]
            gg = g % 16
            return buf[:, gg * CW:(gg + 1) * CW]
        slot_st = [None] * 4
        prev_mm = p.cur("sT")
        for c in range(T // CW):
            c0 = c * CW
            p.wait("sync", "sT", prev_mm)
            lv = []
            for b in range(B):
                for half in range(2):
                    lv.append(dma_load(
                        Bf[2 * b + half][:],
                        s1_s[b, half * 16:(half + 1) * 16, :, c0:c0 + CW]
                        .transpose([1, 0, 2]), f"a{b}"))
            p.wait("tensor", w2c)
            for tok in lv:
                p.wait("tensor", tok)
            for m in range(G):
                for b in range(B):
                    i, bank = banks.acquire()
                    sl = [s1sl(b, g) for g in range(GF)]
                    mm_group(bank[:, 0:CW], w2slices(m), sl,
                             extra=(b2h[0:1, m * 128:(m + 1) * 128], onesrow[0:1, 0:CW]))
                    banks.wrote(i)
                    banks.pre_read(i, "scalar")
                    slot = (m * B + b) % 4
                    if slot_st[slot]:
                        p.wait("scalar", slot_st[slot])
                    v = p.do("scalar", lambda E, o=gt[slot][:, 0:CW], s=bank:
                             E.activation(out=o, in_=s[:, 0:CW], func=AF.Copy, bias=0.0,
                                          scale=0.5), "sA")
                    banks.post_read(i, "sA", v)
                    p.wait("sync", "sA", v)
                    slot_st[slot] = dma_store(h2_s[b, m, :, c0:c0 + CW],
                                              gt[slot][:, 0:CW], f"st{slot}")
            prev_mm = p.cur("sT")

    def phase_ln2_resid(src_dram, dst_dram):
        xbf2 = [Bf[2], Bf[3]]
        xsq2 = [kact_ar[:, 0:G * CH], kact_ar[:, G * CH:2 * G * CH]]
        ffnb = [Bf[0], Bf[1]]
        prev_sv = p.cur("sV")
        prev_st = []
        prev_mm = p.cur("sT")
        for c in range(NCH):
            c0 = c * CH
            p.wait("sync", "sV", prev_sv)
            for tok in prev_st:
                p.wait("sync", tok)
            lv = [dma_load(xch[b][:], xslice(x1_s, b, c0), f"x{b}") for b in range(B)]
            lvf = [dma_load(ffnb[b][:], xslice(ffn_s, b, c0), f"a{b}") for b in range(B)]
            for tok in lv + lvf:
                p.wait("vector", tok)
            for b in range(B):
                p.do("vector", lambda E, o=xch[b][:], a=xch[b][:], f_=ffnb[b][:]:
                     E.tensor_tensor(out=o, in0=a, in1=f_, op=alu.add), "sV")
            xpv = p.cur("sV")
            cvals, svals = [], []
            p.wait("scalar", "sV", xpv)
            p.wait("scalar", "sT", prev_mm)
            for b in range(B):
                cv = p.do("scalar", lambda E, o=xbf2[b][:], a=xch[b][:]:
                          E.activation(out=o, in_=a, func=AF.Copy, bias=0.0, scale=1.0), "sA")
                cvals.append(cv)
            for b in range(B):
                sv = p.do("scalar", lambda E, o=xsq2[b][:], a=xch[b][:]:
                          E.activation(out=o, in_=a, func=AF.Square, bias=0.0, scale=1.0), "sA")
                svals.append(sv)
            bval = ln_stats(xbf2, xsq2, G, cvals, svals)
            p.wait("vector", "sA", bval)
            for m in range(G):
                for b in range(B):
                    p.do("vector", lambda E, o=tp[b][:],
                         a=xch[b][:, m * CH:(m + 1) * CH], r_=rbc[b][:]:
                         E.tensor_tensor(out=o, in0=a, in1=r_, op=alu.mult), None)
                for b in range(B):
                    p.do("vector", lambda E, o=gt[b][:], a=tp[b][:], m_=rmubc[b][:]:
                         E.tensor_tensor(out=o, in0=a, in1=m_, op=alu.subtract), None)
                for b in range(B):
                    p.do("vector", lambda E, o=xch[b][:, m * CH:(m + 1) * CH],
                         a=gt[b][:], g2=vec["ln2g"][:, m:m + 1], bb2=vec["ln2b"][:, m:m + 1]:
                         E.tensor_scalar(out=o, in0=a, scalar1=g2, scalar2=bb2,
                                         op0=alu.mult, op1=alu.add), "sV")
            p.wait("sync", "sT", p.cur("sT"))
            prev_st = []
            for b in range(B):
                lx = dma_load(F[2][:], xslice(src_dram, b, c0), "x2")
                p.wait("vector", lx)
                v = p.do("vector", lambda E, o=xch[b][:], a=xch[b][:], xx=F[2][:]:
                         E.tensor_tensor(out=o, in0=a, in1=xx, op=alu.add), "sV")
                p.wait("sync", "sV", v)
                prev_st.append(dma_store(xslice(dst_dram, b, c0), xch[b][:], f"ox{b}"))
            prev_sv = p.cur("sV")
            prev_mm = p.cur("sT")

    def phase_l1ffn_quant():
        qb = Banks(p, ps[0:4])
        ws1ps = stps
        NW = float(D * 4 * D)
        p.do("gpsimd", lambda E: E.memset(cbacc[:], 0.0), "sG")
        p.wait("vector", "sG", p.cur("sG"))
        for w_i, (wsrc, npiece, pw) in enumerate(((WfcT, 8, 4 * D), (WprojT, 32, D))):
            for k in range(npiece):
                buf = F[k % 2][:, 0:pw]
                p.wait("sync", "sV", p.cur("sV") if k >= 2 else 0)
                lv = dma_load(buf, wsrc[k * 128:(k + 1) * 128, :], f"x{k % 2}")
                p.wait("vector", lv)
                col = (0 if w_i == 0 else 32) + k
                p.do("vector", lambda E, o=lifst[:, col:col + 1], a=buf:
                     E.tensor_reduce(out=o, in_=a, axis=mybir.AxisListType.X,
                                     op=alu.add, apply_absolute_value=True), "sV")
            off = 0 if w_i == 0 else 32
            p.barrier()
            p.do("vector", lambda E, o=lifst[:, 30 + w_i:31 + w_i],
                 a=lifst[:, off:off + npiece]:
                 E.tensor_reduce(out=o, in_=a, axis=mybir.AxisListType.X,
                                 op=alu.add), "sV")
            rv = p.cur("sV")
            i, bank = qb.acquire()
            p.wait("tensor", "sV", rv)
            p.do("tensor", lambda E, bk=bank, w=lifst[:, 30 + w_i:31 + w_i], r=onescolf[:]:
                 E.matmul(bk[0:1, 0:1], w, r, start=True, stop=True), "sT")
            qb.wrote(i)
            qb.pre_read(i, "vector")
            p.do("vector", lambda E, o=throw[0:1, w_i:w_i + 1], s=bank:
                 E.tensor_scalar(out=o, in0=s[0:1, 0:1], scalar1=1.0 / NW,
                                 scalar2=1e-6, op0=alu.mult, op1=alu.add), "sV")
            p.do("vector", lambda E, o=throw[0:1, 2 + w_i:3 + w_i], s=bank:
                 E.tensor_scalar(out=o, in0=s[0:1, 0:1], scalar1=0.5 / NW,
                                 scalar2=0.5e-6, op0=alu.mult, op1=alu.add), "sV")
            v = p.do("vector", lambda E, o=throw[0:1, 4 + w_i:5 + w_i], s=bank:
                     E.tensor_scalar(out=o, in0=s[0:1, 0:1], scalar1=-0.5 / NW,
                                     scalar2=-0.5e-6, op0=alu.mult, op1=alu.add), "sV")
            qb.post_read(i, "sV", v)
        p.barrier()
        thv = p.cur("sV")
        p.wait("tensor", "sV", thv)
        for row_i, dst_i in ((2, 0), (4, 1), (3, 2), (5, 3)):
            i, bank = qb.acquire()
            p.do("tensor", lambda E, bk=bank, r=throw[0:1, row_i:row_i + 1]:
                 E.matmul(bk[:, 0:1], ones1f[:], r, start=True, stop=True), "sT")
            qb.wrote(i)
            qb.pre_read(i, "scalar")
            v = p.do("scalar", lambda E, o=thcol[:, dst_i:dst_i + 1], s=bank:
                     E.activation(out=o, in_=s[:, 0:1], func=AF.Copy, bias=0.0,
                                  scale=1.0), "sA")
            qb.post_read(i, "sA", v)
        p.barrier()
        thav = p.cur("sA")
        p.wait("vector", "sA", thav)
        for w_i, (wsrc, wdst, npiece, pw, fold) in enumerate(
                ((WfcT, wfcq_s, 8, 4 * D, False), (WprojT, wprq_s, 32, D, True))):
            tcp, tcn = (0, 1) if w_i == 0 else (2, 3)
            sqtok = []
            for k0 in range(0, npiece, 2):
                lvv = []
                for kk in range(2):
                    k = k0 + kk
                    p.wait("sync", "sV", p.cur("sV") if k0 >= 2 else 0)
                    p.wait("sync", "sT", p.cur("sT") if k0 >= 2 else 0)
                    lvv.append(dma_load(F[kk][:, 0:pw],
                                        wsrc[k * 128:(k + 1) * 128, :], f"x{kk}"))
                for tok in lvv:
                    p.wait("vector", tok)
                p.wait("vector", "sT", p.cur("sT") if k0 >= 2 else 0)
                for tok in sqtok:
                    p.wait("vector", tok)
                q1 = [Bf[0].bitcast(BF16)[:, 0:pw], Bf[1].bitcast(BF16)[:, 0:pw]]
                q2 = [Bf[2].bitcast(BF16)[:, 0:pw], Bf[3].bitcast(BF16)[:, 0:pw]]
                qo = [kact_ar[:, 0:pw], kact_ar[:, 4 * D:4 * D + pw]]
                for kk in range(2):
                    p.do("vector", lambda E, o=q1[kk], a=F[kk][:, 0:pw],
                         t=thcol[:, tcp:tcp + 1]:
                         E.tensor_scalar(out=o, in0=a, scalar1=t, scalar2=None,
                                         op0=alu.is_gt), None)
                for kk in range(2):
                    p.do("vector", lambda E, o=q2[kk], a=F[kk][:, 0:pw],
                         t=thcol[:, tcn:tcn + 1]:
                         E.tensor_scalar(out=o, in0=a, scalar1=t, scalar2=None,
                                         op0=alu.is_lt), None)
                for kk in range(2):
                    p.do("vector", lambda E, o=qo[kk], a=q1[kk], bb=q2[kk]:
                         E.tensor_tensor(out=o, in0=a, in1=bb, op=alu.subtract),
                         "sV" if kk == 1 else None)
                qv = p.cur("sV")
                if fold:
                    p.wait("tensor", "sV", qv)
                    for kk in range(2):
                        k = k0 + kk
                        for m in range(G):
                            i, bank = qb.acquire()
                            p.do("tensor", lambda E, bk=bank,
                                 w=qo[kk][:, m * 128:(m + 1) * 128],
                                 r=lnb_sb[:, k:k + 1]:
                                 E.matmul(bk[:, 0:1], w, r, start=True, stop=True), "sT")
                            qb.wrote(i)
                            qb.pre_read(i, "vector")
                            v = p.do("vector", lambda E, o=cbacc[:, m:m + 1], s=bank:
                                     E.tensor_tensor(out=o, in0=o, in1=s[:, 0:1],
                                                     op=alu.add), "sV")
                            qb.post_read(i, "sV", v)
                    cbmm = p.cur("sT")
                    p.wait("scalar", "sT", cbmm)
                    for kk in range(2):
                        k = k0 + kk
                        p.do("scalar", lambda E, o=qo[kk], a=qo[kk], s=lnw_sb[:, k:k + 1]:
                             E.activation(out=o, in_=a, func=AF.Copy, bias=0.0,
                                          scale=s), "sA")
                    fav = p.cur("sA")
                    p.wait("tensor", "sA", fav)
                    for kk in range(2):
                        k = k0 + kk
                        for half in range(2):
                            p.do("tensor", lambda E, bk=ws1ps[half], w=ones1b,
                                 r=qo[kk][:, half * CH:(half + 1) * CH],
                                 st=(k == 0), sp=(k == npiece - 1):
                                 E.matmul(bk[:], w[:], r, start=st, stop=sp),
                                 "sT" if half == 1 else None)
                    p.wait("sync", "sA", fav)
                    p.wait("sync", "sT", p.cur("sT"))
                else:
                    p.wait("sync", "sV", qv)
                sqtok = [dma_store(wdst[(k0 + kk) * 128:(k0 + kk + 1) * 128, :],
                                   qo[kk], f"sq{kk}") for kk in range(2)]
        p.barrier()
        dma_store(qrows[:], rowar[:], "dbgq")
        dma_store(qthcol[:], thcol[:], "dbgq")
        dma_store(qlifst[:], lifst[:], "dbgq")
        p.barrier()
        p.wait("scalar", "sV", p.cur("sV"))
        p.wait("scalar", "sT", p.cur("sT"))
        p.do("scalar", lambda E, o=cbh[:], a=cbacc:
             E.activation(out=o, in_=a[:], func=AF.Copy, bias=0.0, scale=0.5), "sA")
        p.wait("vector", "sT", p.cur("sT"))
        for half in range(2):
            p.do("vector", lambda E, o=ws1neg[0:1, half * CH:(half + 1) * CH],
                 s=ws1ps[half]:
                 E.tensor_scalar(out=o, in0=s[0:1, 0:CH], scalar1=-1.0, scalar2=None,
                                 op0=alu.mult), "sV")

    def phase_l1ffn(src_dram, dst_dram):
        bk4 = Banks(p, ps[0:4])
        wl = []
        for k in range(8):
            dst = wbig[:, k * 4 * D:(k + 1) * 4 * D] if k < 6 else                 kact_ar[:, (k - 6) * 4 * D:(k - 5) * 4 * D]
            wl.append(dma_load(dst, wfcq_s[k * 128:(k + 1) * 128, :], "lw"))

        def wfcsl(mglob):
            res = []
            for k in range(8):
                if k < 6:
                    res.append(wbig[:, k * 4 * D + mglob * 128:k * 4 * D + mglob * 128 + 128])
                else:
                    res.append(kact_ar[:, (k - 6) * 4 * D + mglob * 128:
                                       (k - 6) * 4 * D + mglob * 128 + 128])
            return res

        prev_mm = p.cur("sT")
        prev_sv = p.cur("sV")
        kslot_st = [None, None]
        for c in range(NCH):
            c0 = c * CH
            p.wait("sync", "sV", prev_sv)
            lv = [dma_load(xch[b][:], xslice(src_dram, b, c0), f"x{b}") for b in range(B)]
            p.wait("scalar", "sT", prev_mm)
            cvals, svals = [], []
            for b in range(B):
                p.wait("scalar", lv[b])
                cv = p.do("scalar", lambda E, o=xbf[b][:], a=xch[b][:]:
                          E.activation(out=o, in_=a, func=AF.Copy, bias=0.0, scale=1.0), "sA")
                cvals.append(cv)
            for b in range(B):
                sv = p.do("scalar", lambda E, o=xsq[b][:], a=xch[b][:]:
                          E.activation(out=o, in_=a, func=AF.Square, bias=0.0, scale=1.0), "sA")
                svals.append(sv)
            bval = ln_stats(xbf, xsq, G, cvals, svals, trk=bk4)
            for tok_ in kslot_st:
                if tok_:
                    p.wait("vector", tok_)
            xnv = build_xn(xch, bval)
            prev_sv = xnv
            xnsl = [[xn[b][:, g * (CH + 1) + 1:(g + 1) * (CH + 1)] for g in range(G)]
                    for b in range(B)]
            p.wait("tensor", "sV", xnv)
            p.wait("tensor", wl[-1])
            si = 0
            for mglob in range(GF):
                for b in range(B):
                    i, bank = bk4.acquire()
                    mm_group(bank, wfcsl(mglob), xnsl[b])
                    bk4.wrote(i)
                    bk4.pre_read(i, "scalar")
                    slot = si % 4
                    ks = si % 2
                    si += 1
                    p.do("scalar", lambda E, o=gt[slot][:], s=bank:
                             E.activation(out=o, in_=s[:], func=AF.Gelu, bias=0.0,
                                          scale=1.0), None)
                    if kslot_st[ks]:
                        p.wait("scalar", kslot_st[ks])
                    v = p.do("scalar", lambda E, o=kvb[:, ks * CH:(ks + 1) * CH],
                             a=gt[slot][:]:
                             E.activation(out=o, in_=a, func=AF.Copy, bias=0.0,
                                          scale=1.0), "sA")
                    bk4.post_read(i, "sA", v)
                    p.wait("sync", "sA", v)
                    kslot_st[ks] = dma_store(s1_s[b, mglob, :, c0:c0 + CH],
                                             kvb[:, ks * CH:(ks + 1) * CH], f"st{ks}")
            prev_mm = p.cur("sT")
        p.barrier()
        bk4b = Banks(p, ps[0:4])
        stb = Banks(p, stps)
        CW = 256
        wl2 = []
        for k in range(GF):
            if k < 24:
                dst = wbig[:, k * D:(k + 1) * D]
            else:
                dst = kact_ar[:, (k - 24) * D:(k - 23) * D]
            wl2.append(dma_load(dst, wprq_s[k * 128:(k + 1) * 128, :], "lw"))

        def wprsl(m):
            res = [wbig[:, k * D + m * 128:k * D + m * 128 + 128] for k in range(24)]
            res += [kact_ar[:, (k - 24) * D + m * 128:(k - 24) * D + m * 128 + 128]
                    for k in range(24, GF)]
            return res

        def hsl(b, g, cw=CW):
            buf = Bf[2 * b + (g // 16)]
            gg = g % 16
            return buf[:, gg * cw:(gg + 1) * cw]

        prev_mm = p.cur("sT")
        prev_sv2 = p.cur("sV")
        for c in range(T // CW):
            c0 = c * CW
            p.wait("sync", "sT", prev_mm)
            p.wait("sync", "sV", prev_sv2)
            lvh = []
            for b in range(B):
                for half in range(2):
                    lvh.append(dma_load(
                        Bf[2 * b + half][:, 0:16 * CW],
                        s1_s[b, half * 16:(half + 1) * 16, :, c0:c0 + CW]
                        .transpose([1, 0, 2]), f"a{b}"))
            lvx = [dma_load(F[b][:, 0:G * CW], xslice(src_dram, b, c0, CW), f"x{b}")
                   for b in range(B)]
            sums, sqs = [], []
            for tok in lvh:
                p.wait("tensor", tok)
            for b in range(B):
                i, bank = stb.acquire()
                for g in range(GF):
                    p.do("tensor", lambda E, bk=bank, w=ones1b, r=hsl(b, g),
                         st=(g == 0), sp=(g == GF - 1):
                         E.matmul(bk[0:1, 0:CW], w[:], r, start=st, stop=sp),
                         "sT" if g == GF - 1 else None)
                stb.wrote(i)
                sums.append((stb, i, bank))
            p.wait("tensor", "sV", max(banks.state[4]["rval"], banks.state[5]["rval"]))
            for b in range(B):
                for tok in lvh:
                    p.wait("scalar", tok)
                sqmm = {0: 0, 1: 0}
                for g in range(GF):
                    ks = g % 2
                    sqslot = ybuf[:, 2048 + ks * CW:2048 + (ks + 1) * CW]
                    p.wait("scalar", "sT", sqmm[ks])
                    p.do("scalar", lambda E, o=sqslot, a=hsl(b, g):
                         E.activation(out=o, in_=a, func=AF.Square, bias=0.0, scale=1.0),
                         "sA")
                    sv = p.cur("sA")
                    p.wait("tensor", "sA", sv)
                    sqmm[ks] = p.do("tensor", lambda E, bk=ps[4 + b], w=ones1b,
                                    r=sqslot,
                                    st=(g == 0), sp=(g == GF - 1):
                                    E.matmul(bk[0:1, 0:CW], w[:], r, start=st, stop=sp),
                                    "sT")
                sqs.append((banks, 4 + b, ps[4 + b]))
            banks.state[4]["wval"] = p.cur("sT")
            banks.state[5]["wval"] = p.cur("sT")
            bval = ln_finish(sums, sqs, 1.0 / (4 * D), width=CW, trk=bk4b)
            p.wait("tensor", "sA", bval)
            p.wait("tensor", wl2[-1])
            p.wait("vector", "sA", bval)
            for m in range(G):
                hb = []
                for b in range(B):
                    i, bank = bk4b.acquire()
                    mm_group(bank[:, 0:CW], wprsl(m), [hsl(b, g) for g in range(GF)],
                             extra=(ws1neg[0:1, m * 128:(m + 1) * 128],
                                    srow[b][0:1, 0:CW]))
                    bk4b.wrote(i)
                    hb.append((i, bank))
                for b in range(B):
                    i, bank = hb[b]
                    bk4b.pre_read(i, "vector")
                    p.do("vector", lambda E, o=tp[b][:, 0:CW], s=bank,
                         r_=rbc[b][:, 0:CW]:
                         E.tensor_tensor(out=o, in0=s[:, 0:CW], in1=r_, op=alu.mult), None)
                for b in range(B):
                    p.do("vector", lambda E, o=gt[b][:, 0:CW], a=tp[b][:, 0:CW],
                         cb_=cbh[:, m:m + 1]:
                         E.tensor_scalar(out=o, in0=a, scalar1=0.5, scalar2=cb_,
                                         op0=alu.mult, op1=alu.add), None)
                for b in range(B):
                    i, bank = hb[b]
                    p.wait("vector", lvx[b])
                    v = p.do("vector", lambda E, o=F[b][:, m * CW:(m + 1) * CW],
                             a=gt[b][:, 0:CW], xx=F[b][:, m * CW:(m + 1) * CW]:
                             E.tensor_tensor(out=o, in0=a, in1=xx, op=alu.add), "sV")
                    bk4b.post_read(i, "sV", v)
            ov = p.cur("sV")
            p.wait("sync", "sV", ov)
            for b in range(B):
                dma_store(xslice(dst_dram, b, c0, CW), F[b][:, 0:G * CW], f"ox{b}")
            prev_mm = p.cur("sT")
            prev_sv2 = ov

    def phase_l1attn_rkv(src_dram, half):
        bk4 = Banks(p, ps)
        wv = []
        for j in range(3):
            colbase = half * 3 * D + j * D
            ap = rkvT.rearrange("(k p) o -> k p o", p=128)[:, :, colbase:colbase + D]
            ap = ap.transpose([1, 0, 2])
            wv.append(dma_cast(wbig[:, j * G * D:(j + 1) * G * D], ap))
        prev_xv = p.cur("sV")
        prev_mm = p.cur("sT")
        slot_st = [None] * 4
        for c in range(NCH):
            c0 = c * CH
            p.wait("sync", "sV", prev_xv)
            lv = [dma_load(xch[b][:], xslice(src_dram, b, c0), f"x{b}") for b in range(B)]
            p.wait("scalar", "sT", prev_mm)
            cvals, svals = cast_and_square(xch, lv)
            bval = ln_stats(xbf, xsq, G, cvals, svals, trk=bk4)
            xnv = build_xn(xch, bval)
            prev_xv = xnv
            p.wait("tensor", "sV", xnv)
            p.wait("tensor", wv[2])
            xnsl = [[xn[b][:, g * (CH + 1) + 1:(g + 1) * (CH + 1)] for g in range(G)]
                    for b in range(B)]
            si = 0
            for j in range(3):
                for m in range(G):
                    for b in range(B):
                        i, bank = bk4.acquire()
                        mm_group(bank, wslices(j * G * D, m, G, D), xnsl[b])
                        bk4.wrote(i)
                        bk4.pre_read(i, "scalar")
                        slot = si % 4
                        si += 1
                        if slot_st[slot]:
                            p.wait("scalar", slot_st[slot])
                        fn = AF.Sigmoid if j == 0 else AF.Copy
                        v = p.do("scalar", lambda E, o=gt[slot][:], s=bank, f=fn:
                                 E.activation(out=o, in_=s[:], func=f, bias=0.0,
                                              scale=1.0), "sA")
                        bk4.post_read(i, "sA", v)
                        p.wait("sync", "sA", v)
                        slot_st[slot] = dma_store(rkv_s[b, j, m, :, c0:c0 + CH],
                                                  gt[slot][:], f"st{slot}")
            prev_mm = p.cur("sT")

    def phase_wkv(dst_dram, wname, uname, reverse):
        wbigf = wbig.bitcast(F32)
        xnf = [xn[0].bitcast(F32), xn[1].bitcast(F32)]
        TP1 = T + 1
        MX = [wbigf[:, 0:TP1], wbigf[:, TP1:2 * TP1]]
        NUM = [wbigf[:, 2 * TP1:3 * TP1], wbigf[:, 3 * TP1:4 * TP1]]
        DEN = [wbigf[:, 4 * TP1:5 * TP1], xnf[0][:, 0:TP1]]
        Bff = [Bf[i].bitcast(F32)[:, 0:T] for i in range(4)]
        tA = [Bff[0], Bff[1]]
        tB = [Bff[2], Bff[3]]
        kin = [F[0][:, 0:T], F[0][:, T:2 * T]]
        vin = [F[1][:, 0:T], F[1][:, T:2 * T]]
        rin_ = [F[2][:, 0:T], F[2][:, T:2 * T]]
        prev = 0
        out_tok = []
        for g in range(G):
            p.wait("sync", "sV", prev)
            lvs = []
            for b in range(B):
                lvs.append(dma_load(kin[b], rkv_s[b, 1, g, :, :], "lk"))
                lvs.append(dma_load(vin[b], rkv_s[b, 2, g, :, :], "lk"))
                lvs.append(dma_load(rin_[b], rkv_s[b, 0, g, :, :], "lk"))
            p.wait("vector", lvs[-1])
            p.wait("vector", "sA", p.cur("sA"))
            for tok in out_tok:
                p.wait("vector", tok)
            p.wait("gpsimd", "sV", prev)
            wap = vec[wname][:, g:g + 1]
            uap = vec[uname][:, g:g + 1]
            rv = lambda ap_: ap_[:, ::-1] if reverse else ap_
            bcol = T if reverse else 0
            for b in range(B):
                p.do("gpsimd", lambda E, o=MX[b][:, bcol:bcol + 1]: E.memset(o, -1e38), None)
                p.do("gpsimd", lambda E, o=NUM[b][:, bcol:bcol + 1]: E.memset(o, 0.0), None)
            p.do("gpsimd", lambda E, o=DEN[0][:, bcol:bcol + 1]: E.memset(o, 0.0), None)
            gv = p.do("gpsimd", lambda E, o=DEN[1][:, bcol:bcol + 1]: E.memset(o, 0.0), "sG")
            p.wait("vector", "sG", gv)
            if not reverse:
                post = lambda X: X[:, 1:T + 1]
                pre = lambda X: X[:, 0:T]
                ini = lambda X: X[:, 0:1]
            else:
                post = lambda X: X[:, 0:T][:, ::-1]
                pre = lambda X: X[:, 1:T + 1]
                ini = lambda X: X[:, T:T + 1]
            for b in range(B):
                p.do("vector", lambda E, o=post(MX[b]), d0=wap.broadcast_to([128, T]),
                     d1=rv(kin[b]), i_=ini(MX[b]):
                     E.tensor_tensor_scan(out=o, data0=d0, data1=d1, initial=i_,
                                          op0=alu.add, op1=alu.max), "sV")
            MXpost = [MX[b][:, 1:T + 1] if not reverse else MX[b][:, 0:T] for b in range(B)]
            MXpre_n = [MX[b][:, 0:T] if not reverse else MX[b][:, 1:T + 1] for b in range(B)]
            for b in range(B):
                p.do("vector", lambda E, o=tA[b], a=MXpre_n[b], bb=MXpost[b]:
                     E.tensor_tensor(out=o, in0=a, in1=bb, op=alu.subtract), "sV")
            for b in range(B):
                p.do("vector", lambda E, o=tB[b], a=kin[b], bb=MXpost[b]:
                     E.tensor_tensor(out=o, in0=a, in1=bb, op=alu.subtract), "sV")
            dv = p.cur("sV")
            p.wait("scalar", "sV", dv)
            for b in range(B):
                p.do("scalar", lambda E, o=tA[b], a=tA[b], w=wap:
                     E.activation(out=o, in_=a, func=AF.Exp, bias=w, scale=1.0), None)
            for b in range(B):
                p.do("scalar", lambda E, o=tB[b], a=tB[b]:
                     E.activation(out=o, in_=a, func=AF.Exp, bias=0.0, scale=1.0), "sA")
            ev = p.cur("sA")
            p.wait("vector", "sA", ev)
            for b in range(B):
                p.do("vector", lambda E, o=gtbuf[:, 0:T] if b == 0 else bcbuf[:, 0:T],
                     a=tB[b], bb=vin[b]:
                     E.tensor_tensor(out=o, in0=a, in1=bb, op=alu.mult), "sV")
            for b in range(B):
                p.do("vector", lambda E, o=post(NUM[b]), d0=rv(tA[b]),
                     d1=rv(gtbuf[:, 0:T] if b == 0 else bcbuf[:, 0:T]), i_=ini(NUM[b]):
                     E.tensor_tensor_scan(out=o, data0=d0, data1=d1, initial=i_,
                                          op0=alu.mult, op1=alu.add), "sV")
            for b in range(B):
                p.do("vector", lambda E, o=post(DEN[b]), d0=rv(tA[b]), d1=rv(tB[b]),
                     i_=ini(DEN[b]):
                     E.tensor_tensor_scan(out=o, data0=d0, data1=d1, initial=i_,
                                          op0=alu.mult, op1=alu.add), "sV")
            for b in range(B):
                p.do("vector", lambda E, o=tA[b], a=kin[b], u=uap:
                     E.tensor_scalar(out=o, in0=a, scalar1=u, scalar2=None,
                                     op0=alu.add), "sV")
            NUMpre = [NUM[b][:, 0:T] if not reverse else NUM[b][:, 1:T + 1]
                      for b in range(B)]
            DENpre = [DEN[b][:, 0:T] if not reverse else DEN[b][:, 1:T + 1]
                      for b in range(B)]
            for b in range(B):
                p.do("vector", lambda E, o=tB[b], a=MXpre_n[b], bb=tA[b]:
                     E.tensor_tensor(out=o, in0=a, in1=bb, op=alu.max), "sV")
            for b in range(B):
                p.do("vector", lambda E, o=gtbuf[:, 0:T] if b == 0 else bcbuf[:, 0:T],
                     a=MXpre_n[b], bb=tB[b]:
                     E.tensor_tensor(out=o, in0=a, in1=bb, op=alu.subtract), "sV")
            for b in range(B):
                p.do("vector", lambda E, o=tA[b], a=tA[b], bb=tB[b]:
                     E.tensor_tensor(out=o, in0=a, in1=bb, op=alu.subtract), "sV")
            e12 = p.cur("sV")
            p.wait("scalar", "sV", e12)
            for b in range(B):
                p.do("scalar", lambda E, o=gtbuf[:, 0:T] if b == 0 else bcbuf[:, 0:T],
                     a=gtbuf[:, 0:T] if b == 0 else bcbuf[:, 0:T]:
                     E.activation(out=o, in_=a, func=AF.Exp, bias=0.0, scale=1.0), None)
            for b in range(B):
                p.do("scalar", lambda E, o=tA[b], a=tA[b]:
                     E.activation(out=o, in_=a, func=AF.Exp, bias=0.0, scale=1.0), "sA")
            e12a = p.cur("sA")
            p.wait("vector", "sA", e12a)
            for b in range(B):
                p.do("vector", lambda E, o=tB[b], a=gtbuf[:, 0:T] if b == 0
                     else bcbuf[:, 0:T], bb=NUMpre[b]:
                     E.tensor_tensor(out=o, in0=a, in1=bb, op=alu.mult), "sV")
            for b in range(B):
                p.do("vector", lambda E, o=kin[b], a=tA[b], bb=vin[b]:
                     E.tensor_tensor(out=o, in0=a, in1=bb, op=alu.mult), "sV")
            for b in range(B):
                p.do("vector", lambda E, o=tB[b], a=tB[b], bb=kin[b]:
                     E.tensor_tensor(out=o, in0=a, in1=bb, op=alu.add), "sV")
            for b in range(B):
                p.do("vector", lambda E, o=vin[b], a=gtbuf[:, 0:T] if b == 0
                     else bcbuf[:, 0:T], bb=DENpre[b]:
                     E.tensor_tensor(out=o, in0=a, in1=bb, op=alu.mult), "sV")
            for b in range(B):
                p.do("vector", lambda E, o=vin[b], a=vin[b], bb=tA[b]:
                     E.tensor_tensor(out=o, in0=a, in1=bb, op=alu.add), "sV")
            yv = p.cur("sV")
            p.wait("scalar", "sV", yv)
            for b in range(B):
                p.do("scalar", lambda E, o=vin[b], a=vin[b]:
                     E.activation(out=o, in_=a, func=AF.Ln, bias=0.0, scale=1.0), None)
            for b in range(B):
                p.do("scalar", lambda E, o=vin[b], a=vin[b]:
                     E.activation(out=o, in_=a, func=AF.Exp, bias=0.0, scale=-1.0), "sA")
            rcv = p.cur("sA")
            p.wait("vector", "sA", rcv)
            for b in range(B):
                p.do("vector", lambda E, o=tB[b], a=tB[b], bb=vin[b]:
                     E.tensor_tensor(out=o, in0=a, in1=bb, op=alu.mult), "sV")
            for b in range(B):
                p.do("vector", lambda E, o=ybuf[:, b * T:(b + 1) * T], a=tB[b], bb=rin_[b]:
                     E.tensor_tensor(out=o, in0=a, in1=bb, op=alu.mult), "sV")
            prev = p.cur("sV")
            p.wait("sync", "sV", prev)
            out_tok = [dma_store(dst_dram[b, g, :, :], ybuf[:, b * T:(b + 1) * T], "yk")
                       for b in range(B)]

    def phase_l1attn_out(src_dram, dst_dram):
        bk4 = Banks(p, ps)
        wv = load_weight(0, outwT, 2 * D, D)
        prev_sv = p.cur("sV")
        prev_st = []
        for c in range(NCH):
            c0 = c * CH
            p.wait("sync", "sV", prev_sv)
            for tok in prev_st:
                p.wait("sync", tok)
            lvf = [dma_load(Bf[b][:], xslice(y_s, b, c0), f"a{b}") for b in range(B)]
            lvb = [dma_load(Bf[2 + b][:], xslice(outb_s, b, c0), f"b{b}") for b in range(B)]
            lvx = [dma_load(xch[b][:], xslice(src_dram, b, c0), f"x{b}") for b in range(B)]
            p.wait("tensor", wv)
            for tok in lvb + lvf:
                p.wait("tensor", tok)
            for m in range(G):
                hb = []
                for b in range(B):
                    i, bank = bk4.acquire()
                    sl = [Bf[b][:, g * CH:(g + 1) * CH] for g in range(G)]
                    sl += [Bf[2 + b][:, g * CH:(g + 1) * CH] for g in range(G)]
                    mm_group(bank, wslices(0, m, 16, D), sl)
                    bk4.wrote(i)
                    hb.append((i, bank))
                for b in range(B):
                    i, bank = hb[b]
                    bk4.pre_read(i, "vector")
                    p.wait("vector", lvx[b])
                    v = p.do("vector", lambda E, o=xch[b][:, m * CH:(m + 1) * CH],
                             s=bank, xx=xch[b][:, m * CH:(m + 1) * CH]:
                             E.tensor_tensor(out=o, in0=s[:], in1=xx, op=alu.add), "sV")
                    bk4.post_read(i, "sV", v)
            ov = p.cur("sV")
            p.wait("sync", "sV", ov)
            prev_st = [dma_store(xslice(dst_dram, b, c0), xch[b][:], f"ox{b}")
                       for b in range(B)]
            prev_sv = ov

    def phase_dbg(src_dram, dst_dram, level):
        wv = [load_weight(0, fWrT, D, D), load_weight(G * D, fWkT, D, D),
              load_weight(2 * G * D, fWvT, D, D)]
        p.do("gpsimd", lambda E, o=xn[0][:, 0::CH + 1]: E.memset(o, 0.0), None)
        p.do("gpsimd", lambda E, o=xn[1][:, 0::CH + 1]: E.memset(o, 0.0), "sG")
        guard_ready = p.cur("sG")
        for c in range(NCH):
            c0 = c * CH
            p.barrier()
            lv = [dma_load(xch[b][:], xslice(src_dram, b, c0), f"x{b}") for b in range(B)]
            cvals, svals = cast_and_square(xch, lv)
            bval = ln_stats(xbf, xsq, G, cvals, svals)
            if c > 0:
                for b in range(B):
                    p.do("vector", lambda E, o=xn[b][:, 0::CH + 1], a=xn[b][:, CH::CH + 1]:
                         E.tensor_copy(out=o, in_=a), "sV")
            else:
                p.wait("vector", "sG", guard_ready)
            build_xn(xch, bval)
            if level == 1:
                p.barrier()
                for b in range(B):
                    for g in range(G):
                        dma_cast(dst_dram[b, g, :, c0:c0 + CH],
                                 xn[b][:, g * (CH + 1) + 1:(g + 1) * (CH + 1)], "dbg")
                continue
            p.wait("vector", "sT", p.cur("sT"))
            for g in range(G):
                for b in range(B):
                    p.do("vector", lambda E, o=tp[b][:],
                         a=xn[b][:, g * (CH + 1):g * (CH + 1) + CH], m1=vec["mur1"][:, g:g + 1]:
                         E.tensor_scalar(out=o, in0=a, scalar1=m1, scalar2=None, op0=alu.mult), "sV")
                for b in range(B):
                    p.do("vector", lambda E, o=mixr[b][:, g * CH:(g + 1) * CH],
                         a=xn[b][:, g * (CH + 1) + 1:(g + 1) * (CH + 1)],
                         m=vec["mur"][:, g:g + 1], t=tp[b][:]:
                         E.scalar_tensor_tensor(out=o, in0=a, scalar=m, in1=t,
                                                op0=alu.mult, op1=alu.add), "sV")
            for g in range(G):
                for b in range(B):
                    p.do("vector", lambda E, o=tp[b][:],
                         a=xn[b][:, g * (CH + 1):g * (CH + 1) + CH], m1=vec["muk1"][:, g:g + 1]:
                         E.tensor_scalar(out=o, in0=a, scalar1=m1, scalar2=None, op0=alu.mult), "sV")
                for b in range(B):
                    p.do("vector", lambda E, o=mixk[b][:, g * CH:(g + 1) * CH],
                         a=xn[b][:, g * (CH + 1) + 1:(g + 1) * (CH + 1)],
                         m=vec["muk"][:, g:g + 1], t=tp[b][:]:
                         E.scalar_tensor_tensor(out=o, in0=a, scalar=m, in1=t,
                                                op0=alu.mult, op1=alu.add), "sV")
            if level in (2, 3):
                p.barrier()
                srcb = mixr if level == 2 else mixk
                for b in range(B):
                    dma_cast(xslice(dst_dram, b, c0), srcb[b][:], "dbg")
                continue
            mixv = p.cur("sV")
            p.wait("tensor", wv[2])
            p.wait("tensor", "sV", mixv)
            for m in range(G):
                for b in range(B):
                    i, bank = banks.acquire()
                    mm_group(bank, wslices(G * D, m, G, D), rslices(mixk[b], G))
                    banks.wrote(i)
                    banks.pre_read(i, "scalar")
                    tslot = gt[(m % 2) * 2 + b]
                    v = p.do("scalar", lambda E, o=tslot[:], s=bank:
                             E.activation(out=o, in_=s[:], func=AF.Relu, bias=0.0,
                                          scale=1.0), "sA")
                    banks.post_read(i, "sA", v)
                    if level == 4:
                        p.barrier()
                        dma_store(dst_dram[b, m, :, c0:c0 + CH], tslot[:], "dbg")
                        p.barrier()
            if level == 4:
                continue
            for m in range(G):
                for b in range(B):
                    i, bank = banks.acquire()
                    mm_group(bank, wslices(0, m, G, D), rslices(mixr[b], G))
                    banks.wrote(i)
                    banks.pre_read(i, "scalar")
                    gslot = gt[(m % 2) * 2 + b]
                    v = p.do("scalar", lambda E, o=gslot[:], s=bank:
                             E.activation(out=o, in_=s[:], func=AF.Sigmoid, bias=0.0,
                                          scale=1.0), "sA")
                    banks.post_read(i, "sA", v)
                    p.barrier()
                    dma_store(dst_dram[b, m, :, c0:c0 + CH], gslot[:], "dbg")
                    p.barrier()

    phase_list = [
        (phase0, None),
        (lambda: phase_l0ffn(x_in, xresB), xresB),
        (lambda: phase_l0attn_rkv(xresB), xresB),
        (phase_mixer, xresB),
        (lambda: phase_x1(xresB), xresB),
        (phase_w1, xresB),
        (lambda: phase_lif(h1_s, s1_s, GF, 64), xresB),
        (phase_w2, xresB),
        (lambda: phase_lif(h2_s, ffn_s, G, LC), xresB),
        (lambda: phase_ln2_resid(xresB, xresA), xresA),
        (lambda: phase_l0ffn(xresA, xresB), xresB),
        (phase_l1ffn_quant, xresB),
        (lambda: phase_l1ffn(xresB, xresA), xresA),
        (lambda: phase_l1attn_rkv(xresA, 1), xresA),
        (lambda: phase_wkv(outb_s, "wr", "ur", True), xresA),
        (lambda: phase_l1attn_rkv(xresA, 0), xresA),
        (lambda: phase_wkv(y_s, "wf", "uf", False), xresA),
        (lambda: phase_l1attn_out(xresA, xresB), xresB),
        (lambda: phase_l1ffn(xresB, xresA), xresA),
    ]
    n_real = len(phase_list)
    phase_list += [
        (lambda: phase_dbg(x_in, xresA, 1), xresA),
        (lambda: phase_dbg(x_in, xresA, 2), xresA),
        (lambda: phase_dbg(x_in, xresA, 3), xresA),
        (lambda: phase_dbg(x_in, xresA, 4), xresA),
        (lambda: phase_dbg(x_in, xresA, 5), xresA),
    ]
    if _PHASE_SEQ is not None:
        seq = [phase_list[i] for i in _PHASE_SEQ]
    else:
        n = n_real if limit_phases is None else min(limit_phases, n_real)
        seq = phase_list[:n]
    cur_src = x_in
    for fn, dst in seq:
        p.barrier()
        fn()
        if dst is not None:
            cur_src = dst
    p.barrier()

    stok = None
    for b in range(B):
        for c in range(NCH):
            c0 = c * CH
            if stok:
                p.wait("sync", stok)
            lv = dma_load(xch[0][:], xslice(cur_src, b, c0), "x0")
            p.wait("sync", lv)
            stok = dma_store(xslice(out_d, b, c0), xch[0][:], "o")
    p.wait("sync", stok)

    ttok = None
    for nm in _DEBUG_TAPS:
        t = dram[nm]
        dbg = nc.dram_tensor("dbg_" + nm, list(t.shape), t.dtype, kind="ExternalOutput").ap()
        if ttok:
            p.wait("sync", ttok)
        ttok = dma_store(dbg, t[:], "o")
    if ttok:
        p.wait("sync", ttok)

    with nc.Block() as blk:
        @blk.sync
        def _(E):
            for f in p.q["sync"]:
                f(E)

        @blk.tensor
        def _(E):
            for f in p.q["tensor"]:
                f(E)

        @blk.vector
        def _(E):
            for f in p.q["vector"]:
                f(E)

        @blk.scalar
        def _(E):
            for f in p.q["scalar"]:
                f(E)

        @blk.gpsimd
        def _(E):
            for f in p.q["gpsimd"]:
                f(E)

    es.close()
    return nc


def _fm(x):
    b, t, d = x.shape
    return np.ascontiguousarray(x.reshape(b, t, d // 128, 128).transpose(0, 2, 3, 1))


def kernel(**inputs):
    ins = {k: np.asarray(v) for k, v in inputs.items()}
    x = _fm(np.asarray(ins["L0_attn_x"], dtype=np.float32))

    def t2(a):
        return np.ascontiguousarray(np.asarray(a, dtype=np.float32).T)

    def asrows(vec1d, g):
        return np.ascontiguousarray(np.asarray(vec1d, dtype=np.float32).reshape(g, 128).T)

    shared = {
        "l0aWrT": t2(ins["L0_attn_Wr"]), "l0aWkT": t2(ins["L0_attn_Wk"]),
        "l0aWvT": t2(ins["L0_attn_Wv"]), "l0aWoT": t2(ins["L0_attn_Wo"]),
        "l0aW1T": t2(ins["L0_attn_W1"]), "l0aW2T": t2(ins["L0_attn_W2"]),
        "l0ab1": np.asarray(ins["L0_attn_b1"], dtype=np.float32).reshape(1, -1),
        "l0ab2": np.asarray(ins["L0_attn_b2"], dtype=np.float32).reshape(1, -1),
        "l0aln1g": asrows(ins["L0_attn_ln1_g"], G), "l0aln1b": asrows(ins["L0_attn_ln1_b"], G),
        "l0aln2g": asrows(ins["L0_attn_ln2_g"], G), "l0aln2b": asrows(ins["L0_attn_ln2_b"], G),
        "l0fWrT": t2(ins["L0_ffn_W_r"]), "l0fWkT": t2(ins["L0_ffn_W_k"]),
        "l0fWvT": t2(ins["L0_ffn_W_v"]),
        "l0fmur": asrows(ins["L0_ffn_mu_r"], G), "l0fmuk": asrows(ins["L0_ffn_mu_k"], G),
        "l1arkvT": t2(ins["L1_attn_rkv_w"]), "l1aoutwT": t2(ins["L1_attn_out_w"]),
        "l1atd": asrows(ins["L1_attn_time_decay"], G),
        "l1atf": asrows(ins["L1_attn_time_first"], G),
        "l1atdr": asrows(ins["L1_attn_time_decay_rev"], G),
        "l1atfr": asrows(ins["L1_attn_time_first_rev"], G),
        "l1fWfcT": t2(ins["L1_ffn_W_fc"]), "l1fWprojT": t2(ins["L1_ffn_W_proj"]),
        "l1flnw": asrows(ins["L1_ffn_ln_w"], GF), "l1flnb": asrows(ins["L1_ffn_ln_b"], GF),
    }

    nc = build_program(_LIMIT_PHASES)
    in_maps = []
    for core in range(NCORES):
        m = dict(shared)
        m["x"] = np.ascontiguousarray(x[core * B:(core + 1) * B])
        in_maps.append(m)
    res = run_bass_kernel_spmd(nc, in_maps, core_ids=list(range(NCORES)))
    global _LAST_RESULTS
    _LAST_RESULTS = res.results
    parts = [res.results[i]["out"] for i in range(NCORES)]
    full = np.concatenate(parts, axis=0)
    return np.ascontiguousarray(full.transpose(0, 3, 1, 2).reshape(16, T, D)).astype(np.float32)


# version 06
# speedup vs baseline: 1.0290x; 1.0290x over previous
import contextlib
import numpy as np

import concourse.bass as bass
import concourse.mybir as mybir
from concourse.bass_utils import run_bass_kernel_spmd
from concourse.alu_op_type import AluOpType as alu

F32 = mybir.dt.float32
BF16 = mybir.dt.bfloat16
AF = mybir.ActivationFunctionType

NCORES = 8
B = 2
T = 2048
D = 1024
G = 8
GF = 32
CH = 512
NCH = T // CH
LC = 128
EPS = 1e-5

ENGINES = ("sync", "tensor", "vector", "scalar", "gpsimd")

_LIMIT_PHASES = None
_DEBUG_TAPS = ()
_PHASE_SEQ = None
_LAST_RESULTS = None


class Prog:
    def __init__(self, nc, es):
        self.nc = nc
        self.es = es
        self.q = {e: [] for e in ENGINES}
        self.cnt = {}
        self.lastw = {e: {} for e in ENGINES}
        self.sems = {}
        for name in ("sT", "sV", "sA", "sG"):
            self.sems[name] = es.enter_context(nc.semaphore(name))
            self.cnt[name] = 0

    def chan(self, name):
        if name not in self.sems:
            self.sems[name] = self.es.enter_context(self.nc.semaphore(name))
            self.cnt[name] = 0
        return name

    def cur(self, sem):
        return self.cnt[sem]

    def wait(self, eng, sem, val=None):
        if isinstance(sem, tuple):
            sem, val = sem
        if val is None or val <= 0:
            return
        if self.lastw[eng].get(sem, 0) >= val:
            return
        self.lastw[eng][sem] = val
        h = self.sems[sem]
        self.q[eng].append(lambda E, h=h, v=val: E.wait_ge(h, v))

    def barrier(self):
        vals = {s: self.cnt[s] for s in self.sems}
        for eng in ENGINES:
            for s, v in vals.items():
                self.wait(eng, s, v)

    def do(self, eng, fn, sem=None, k=1):
        if sem is None:
            self.q[eng].append(fn)
            return None
        self.cnt[sem] += k
        h = self.sems[sem]
        self.q[eng].append(lambda E, fn=fn, h=h, k=k: fn(E).then_inc(h, k))
        return self.cnt[sem]


class Banks:
    def __init__(self, p, tensors):
        self.p = p
        self.t = tensors
        self.n = len(tensors)
        self.state = [dict(rsem=None, rval=0, wval=0) for _ in tensors]
        self.rr = 0

    def acquire(self):
        i = self.rr % self.n
        self.rr += 1
        s = self.state[i]
        if s["rsem"] is not None:
            self.p.wait("tensor", s["rsem"], s["rval"])
        return i, self.t[i]

    def wrote(self, i):
        self.state[i]["wval"] = self.p.cur("sT")

    def pre_read(self, i, eng):
        self.p.wait(eng, "sT", self.state[i]["wval"])

    def post_read(self, i, sem, val):
        self.state[i]["rsem"] = sem
        self.state[i]["rval"] = val


def build_program(limit_phases=None):
    nc = bass.Bass("TRN2", target_bir_lowering=False)
    es = contextlib.ExitStack()

    dram = {}

    def din(name, shape, dt=F32):
        dram[name] = nc.dram_tensor(name, list(shape), dt, kind="ExternalInput").ap()
        return dram[name]

    def dscratch(name, shape, dt=F32):
        dram[name] = nc.dram_tensor(name, list(shape), dt, kind="Internal").ap()
        return dram[name]

    x_in = din("x", [B, G, 128, T])
    out_d = nc.dram_tensor("out", [B, G, 128, T], F32, kind="ExternalOutput").ap()

    WrT = din("l0aWrT", [D, D]); WkT = din("l0aWkT", [D, D])
    WvT = din("l0aWvT", [D, D]); WoT = din("l0aWoT", [D, D])
    W1T = din("l0aW1T", [D, 4 * D]); W2T = din("l0aW2T", [4 * D, D])
    b1_d = din("l0ab1", [1, 4 * D]); b2_d = din("l0ab2", [1, D])
    ln1g_d = din("l0aln1g", [128, G]); ln1b_d = din("l0aln1b", [128, G])
    ln2g_d = din("l0aln2g", [128, G]); ln2b_d = din("l0aln2b", [128, G])
    fWrT = din("l0fWrT", [D, D]); fWkT = din("l0fWkT", [D, D]); fWvT = din("l0fWvT", [D, D])
    mur_d = din("l0fmur", [128, G]); muk_d = din("l0fmuk", [128, G])
    rkvT = din("l1arkvT", [D, 6 * D]); outwT = din("l1aoutwT", [2 * D, D])
    td_d = din("l1atd", [128, G]); tf_d = din("l1atf", [128, G])
    tdr_d = din("l1atdr", [128, G]); tfr_d = din("l1atfr", [128, G])
    WfcT = din("l1fWfcT", [D, 4 * D]); WprojT = din("l1fWprojT", [4 * D, D])
    lnw_d = din("l1flnw", [128, GF]); lnb_d = din("l1flnb", [128, GF])

    xresA = dscratch("xresA", [B, G, 128, T])
    xresB = dscratch("xresB", [B, G, 128, T])
    rkv_s = dscratch("rkv_s", [B, 3, G, 128, T], BF16)
    y_s = dscratch("y_s", [B, G, 128, T], BF16)
    x1_s = dscratch("x1_s", [B, G, 128, T])
    h1_s = dscratch("h1_s", [B, GF, 128, T], BF16)
    s1_s = dscratch("s1_s", [B, GF, 128, T], BF16)
    h2_s = dscratch("h2_s", [B, G, 128, T], BF16)
    ffn_s = dscratch("ffn_s", [B, G, 128, T], BF16)
    outb_s = dscratch("outb_s", [B, G, 128, T], BF16)
    wfcq_s = dscratch("wfcq_s", [D, 4 * D], BF16)
    wprq_s = dscratch("wprq_s", [4 * D, D], BF16)
    qrows = dscratch("qrows", [1, 7680])
    qthcol = dscratch("qthcol", [128, 4])
    qlifst = dscratch("qlifst", [128, 64])

    def sb(name, shape, dt=F32):
        return es.enter_context(nc.sbuf_tensor(name, list(shape), dt))

    ones1b = sb("ones1b", [128, 1], BF16)
    ones1f = sb("ones1f", [1, 128], F32)
    onescolf = sb("onescolf", [128, 1], F32)
    eps11 = sb("eps11", [1, 1], F32)
    vec = {}
    for nm in ("mur", "muk", "mur1", "muk1", "ln1g", "ln1b", "ln2g", "ln2b",
               "wf", "uf", "wr", "ur"):
        vec[nm] = sb("v_" + nm, [128, G], F32)
    lnw_sb = sb("v_lnw", [128, GF], F32)
    lnb_sb = sb("v_lnb", [128, GF], BF16)
    rowar = sb("rowar", [1, 7680], F32)
    b1h = rowar[0:1, 0:4 * D]
    b2h = rowar[0:1, 4 * D:5 * D]
    onesrow = rowar[0:1, 5 * D:5 * D + CH]
    ws1neg = rowar[0:1, 5 * D + CH:5 * D + CH + D]
    throw = rowar[0:1, 5 * D + CH + D:5 * D + CH + D + 8]
    srow = [sb(f"srow{i}", [1, CH], F32) for i in range(6)]

    wbig = sb("wbig", [128, 3 * G * D], BF16)

    F = [sb(f"F{i}", [128, G * CH], F32) for i in range(3)]
    Bf = [sb(f"Bf{i}", [128, G * CH], BF16) for i in range(4)]
    xn = [sb(f"xn{b}", [128, G * (CH + 1)], BF16) for b in range(B)]
    tpbuf = sb("tpbuf", [128, 2 * CH], F32)
    tp = [tpbuf[:, b * CH:(b + 1) * CH] for b in range(B)]
    gtbuf = sb("gtbuf", [128, 4 * CH], F32)
    gt = [gtbuf[:, i * CH:(i + 1) * CH] for i in range(4)]
    bcbuf = sb("bcbuf", [128, 4 * CH], F32)
    rbc = [bcbuf[:, b * CH:(b + 1) * CH] for b in range(B)]
    rmubc = [bcbuf[:, (2 + b) * CH:(3 + b) * CH] for b in range(B)]

    gtbb = gtbuf.bitcast(BF16)
    gtb = [gtbb[:, s0 * 1024:s0 * 1024 + CH] for s0 in range(4)]
    c09 = sb("c09", [128, 1], F32)
    cbh = sb("cbh", [128, G], F32)
    cbacc = sb("cbacc", [128, G], F32)
    thcol = sb("thcol", [128, 4], F32)

    lifst = sb("lifst", [128, 64], F32)
    ybuf = gtbuf.bitcast(BF16)
    kvb = tpbuf.bitcast(BF16)

    xch = [F[0], F[1]]
    kact_ar = F[2].bitcast(BF16)
    kact = [kact_ar[:, b * G * CH:(b + 1) * G * CH] for b in range(B)]
    xbf = [Bf[0], Bf[1]]
    xsq = [Bf[2], Bf[3]]
    mixr = [Bf[0], Bf[1]]
    mixk = [Bf[2], Bf[3]]

    ps = [es.enter_context(nc.psum_tensor(f"ps{i}", [128, CH], F32)) for i in range(6)]
    stps = [es.enter_context(nc.psum_tensor(f"stps{i}", [1, CH], F32)) for i in range(2)]

    p = Prog(nc, es)
    banks = Banks(p, ps)
    stbanks = Banks(p, stps)

    def dma_load(dst_ap, src_ap, ch):
        c = p.chan("L" + ch)
        v = p.do("sync", lambda E, d=dst_ap, s=src_ap: E.dma_start(out=d, in_=s), c, 16)
        return (c, v)

    def dma_store(dst_ap, src_ap, ch):
        c = p.chan("S" + ch)
        v = p.do("sync", lambda E, d=dst_ap, s=src_ap: E.dma_start(out=d, in_=s), c, 16)
        return (c, v)

    def dma_cast(dst_ap, src_ap, ch="w"):
        c = p.chan("W" + ch)
        v = p.do("gpsimd", lambda E, d=dst_ap, s=src_ap: E.dma_start(out=d, in_=s), c, 16)
        return (c, v)

    def xslice(dram_t, b, c0, w=CH):
        return dram_t[b, :, :, c0:c0 + w].transpose([1, 0, 2])

    def xslice3(dram_t, b, j, c0, w=CH):
        return dram_t[b, j, :, :, c0:c0 + w].transpose([1, 0, 2])

    def load_weight(dst_off, src, din_, dout, dst_buf=None):
        dbuf = dst_buf if dst_buf is not None else wbig
        nkt = din_ // 128
        ap = src.rearrange("(k p) o -> k p o", p=128).transpose([1, 0, 2])
        return dma_cast(dbuf[:, dst_off:dst_off + nkt * dout], ap)

    def cast_and_square(src, load_vals, also_sq=True):
        cvals, svals = [], []
        for b in range(B):
            p.wait("scalar", load_vals[b])
            cv = p.do("scalar", lambda E, o=xbf[b][:], a=src[b][:]:
                      E.activation(out=o, in_=a, func=AF.Copy, bias=0.0, scale=1.0), "sA")
            cvals.append(cv)
        if also_sq:
            for b in range(B):
                sv = p.do("scalar", lambda E, o=xsq[b][:], a=src[b][:]:
                          E.activation(out=o, in_=a, func=AF.Square, bias=0.0, scale=1.0), "sA")
                svals.append(sv)
        return cvals, svals

    def ln_evict(handles, dsts, inv, width):
        for b in range(B):
            trk, i, bank = handles[b]
            trk.pre_read(i, "vector")
            v = p.do("vector", lambda E, o=dsts[b], s=bank, iv=inv, w=width:
                     E.tensor_scalar(out=o, in0=s[0:1, 0:w], scalar1=iv, scalar2=None,
                                     op0=alu.mult), "sV")
            trk.post_read(i, "sV", v)

    def ln_finish(sums, sqs, inv, width=CH, trk=None):
        mus = [srow[b][0:1, 0:width] for b in range(B)]
        tmps = [srow[2 + b][0:1, 0:width] for b in range(B)]
        ln_evict(sums, mus, inv, width)
        ln_evict(sqs, tmps, inv, width)
        return ln_rest(width, trk if trk is not None else banks)

    def ln_rest(width, trk):
        mus = [srow[b][0:1, 0:width] for b in range(B)]
        tmps = [srow[2 + b][0:1, 0:width] for b in range(B)]
        rs = [srow[4 + b][0:1, 0:width] for b in range(B)]
        for b in range(B):
            p.do("vector", lambda E, o=rs[b], a=mus[b]:
                 E.tensor_tensor(out=o, in0=a, in1=a, op=alu.mult), "sV")
        for b in range(B):
            p.do("vector", lambda E, o=tmps[b], a=tmps[b], bb=rs[b]:
                 E.tensor_tensor(out=o, in0=a, in1=bb, op=alu.subtract), "sV")
        p.wait("scalar", "sV", p.cur("sV"))
        for b in range(B):
            p.do("scalar", lambda E, o=tmps[b], a=tmps[b]:
                 E.activation(out=o, in_=a, func=AF.Ln, bias=eps11[:], scale=1.0), None)
        for b in range(B):
            p.do("scalar", lambda E, o=rs[b], a=tmps[b]:
                 E.activation(out=o, in_=a, func=AF.Exp, bias=0.0, scale=-0.5), "sA")
        av = p.cur("sA")
        p.wait("vector", "sA", av)
        for b in range(B):
            p.do("vector", lambda E, o=tmps[b], a=rs[b], bb=mus[b]:
                 E.tensor_tensor(out=o, in0=a, in1=bb, op=alu.mult), "sV")
        rmuv = p.cur("sV")
        p.wait("tensor", "sA", av)
        p.wait("tensor", "sV", rmuv)
        p.wait("scalar", "sV", p.cur("sV"))
        for b in range(B):
            for row, dst in ((rs[b], rbc[b]), (tmps[b], rmubc[b])):
                i, bank = trk.acquire()
                p.do("tensor", lambda E, bk=bank, w=ones1f, r=row, wd=width:
                     E.matmul(bk[:, 0:wd], w[:], r, start=True, stop=True), "sT")
                trk.wrote(i)
                trk.pre_read(i, "scalar")
                v = p.do("scalar", lambda E, o=dst, s=bank, wd=width:
                         E.activation(out=o[:, 0:wd], in_=s[:, 0:wd], func=AF.Copy,
                                      bias=0.0, scale=1.0), "sA")
                trk.post_read(i, "sA", v)
        return p.cur("sA")

    def ln_stats(src_bf, src_sq, ntile, cast_vals, sq_vals, trk=None):
        inv = 1.0 / (ntile * 128)
        mus = [srow[b][0:1, 0:CH] for b in range(B)]
        tmps = [srow[2 + b][0:1, 0:CH] for b in range(B)]
        for key, (src, vals, dsts) in (("s", (src_bf, cast_vals, mus)),
                                       ("q", (src_sq, sq_vals, tmps))):
            hl = []
            for b in range(B):
                i, bank = stbanks.acquire()
                p.wait("tensor", "sA", vals[b])
                for g in range(ntile):
                    p.do("tensor", lambda E, bk=bank, w=ones1b,
                         r=src[b][:, g * CH:(g + 1) * CH],
                         st=(g == 0), sp=(g == ntile - 1):
                         E.matmul(bk[:], w[:], r, start=st, stop=sp),
                         "sT" if g == ntile - 1 else None)
                stbanks.wrote(i)
                hl.append((stbanks, i, bank))
            ln_evict(hl, dsts, inv, CH)
        return ln_rest(CH, trk if trk is not None else banks)

    def build_xn(src, bcast_val, dst=None):
        dstb = dst if dst is not None else xn
        p.wait("vector", "sA", bcast_val)
        for g in range(G):
            for b in range(B):
                p.do("vector", lambda E, o=tp[b][:],
                     a=src[b][:, g * CH:(g + 1) * CH], r_=rbc[b][:]:
                     E.tensor_tensor(out=o, in0=a, in1=r_, op=alu.mult), "sV")
            for b in range(B):
                p.do("vector", lambda E, o=dstb[b][:, g * (CH + 1) + 1:(g + 1) * (CH + 1)],
                     a=tp[b][:], m=rmubc[b][:]:
                     E.tensor_tensor(out=o, in0=a, in1=m, op=alu.subtract), "sV")
        return p.cur("sV")

    def mm_group(bank_ap, lhs_slices, rhs_slices, extra=None):
        n = len(lhs_slices)
        for g in range(n):
            last = (g == n - 1) and extra is None
            p.do("tensor", lambda E, bk=bank_ap, w=lhs_slices[g], r=rhs_slices[g],
                 st=(g == 0), sp=last:
                 E.matmul(bk[:], w, r, start=st, stop=sp),
                 "sT" if last else None)
        if extra is not None:
            wx, rx = extra
            p.do("tensor", lambda E, bk=bank_ap, w=wx, r=rx:
                 E.matmul(bk[:], w, r, start=False, stop=True), "sT")
        return p.cur("sT")

    def wslices(off, m, nkt, dout, buf=None):
        bf = buf if buf is not None else wbig
        return [bf[:, off + g * dout + m * 128: off + g * dout + m * 128 + 128]
                for g in range(nkt)]

    def rslices(buf_b, nkt):
        return [buf_b[:, g * CH:(g + 1) * CH] for g in range(nkt)]

    def phase0():
        p.do("gpsimd", lambda E: E.memset(ones1b[:], 1.0), None)
        p.do("gpsimd", lambda E: E.memset(ones1f[:], 1.0), None)
        p.do("gpsimd", lambda E: E.memset(onesrow[:], 1.0), None)
        p.do("gpsimd", lambda E: E.memset(onescolf[:], 1.0), None)
        p.do("gpsimd", lambda E: E.memset(eps11[:], EPS), "sG")
        for nm, d_ in (("mur", mur_d), ("muk", muk_d), ("ln1g", ln1g_d), ("ln1b", ln1b_d),
                       ("ln2g", ln2g_d), ("ln2b", ln2b_d), ("uf", tf_d), ("ur", tfr_d),
                       ("wf", td_d), ("wr", tdr_d)):
            dma_load(vec[nm][:], d_[:], "sm")
        dma_load(lnw_sb[:], lnw_d[:], "sm")
        dma_load(b1h[:], b1_d[:], "sm")
        ld_all = dma_load(b2h[:], b2_d[:], "sm")
        dma_cast(lnb_sb[:], lnb_d[:])
        p.wait("vector", ld_all)
        for nm, src_ in (("mur1", "mur"), ("muk1", "muk")):
            p.do("vector", lambda E, o=vec[nm][:], a=vec[src_][:]:
                 E.tensor_scalar(out=o, in0=a, scalar1=-1.0, scalar2=1.0,
                                 op0=alu.mult, op1=alu.add), "sV")
        p.wait("scalar", ld_all)
        for nm in ("wf", "wr"):
            p.do("scalar", lambda E, o=vec[nm][:], a=vec[nm][:]:
                 E.activation(out=o, in_=a, func=AF.Exp, bias=0.0, scale=1.0), "sA")
        sv = p.cur("sA")
        p.wait("vector", "sA", sv)
        for nm in ("wf", "wr"):
            p.do("vector", lambda E, o=vec[nm][:], a=vec[nm][:]:
                 E.tensor_scalar(out=o, in0=a, scalar1=-1.0, scalar2=None, op0=alu.mult), "sV")

    def phase_l0ffn(src_dram, dst_dram):
        wv = [load_weight(0, fWrT, D, D), load_weight(G * D, fWkT, D, D),
              load_weight(2 * G * D, fWvT, D, D)]
        p.do("gpsimd", lambda E, o=xn[0][:, 0::CH + 1]: E.memset(o, 0.0), None)
        p.do("gpsimd", lambda E, o=xn[1][:, 0::CH + 1]: E.memset(o, 0.0), "sG")
        guard_ready = p.cur("sG")
        prev_ov = p.cur("sV")
        prev_st = []
        prev_mm = p.cur("sT")
        for c in range(NCH):
            c0 = c * CH
            p.wait("sync", "sV", prev_ov)
            for tok in prev_st:
                p.wait("sync", tok)
            lv = [dma_load(xch[b][:], xslice(src_dram, b, c0), f"x{b}") for b in range(B)]
            p.wait("scalar", "sT", prev_mm)
            cvals, svals = cast_and_square(xch, lv)
            bval = ln_stats(xbf, xsq, G, cvals, svals)
            if c > 0:
                for b in range(B):
                    p.do("vector", lambda E, o=xn[b][:, 0::CH + 1], a=xn[b][:, CH::CH + 1]:
                         E.tensor_copy(out=o, in_=a), "sV")
            else:
                p.wait("vector", "sG", guard_ready)
            build_xn(xch, bval)
            p.wait("vector", "sT", p.cur("sT"))
            for g in range(G):
                for b in range(B):
                    p.do("vector", lambda E, o=tp[b][:],
                         a=xn[b][:, g * (CH + 1):g * (CH + 1) + CH], m1=vec["mur1"][:, g:g + 1]:
                         E.tensor_scalar(out=o, in0=a, scalar1=m1, scalar2=None, op0=alu.mult), "sV")
                for b in range(B):
                    p.do("vector", lambda E, o=mixr[b][:, g * CH:(g + 1) * CH],
                         a=xn[b][:, g * (CH + 1) + 1:(g + 1) * (CH + 1)],
                         m=vec["mur"][:, g:g + 1], t=tp[b][:]:
                         E.scalar_tensor_tensor(out=o, in0=a, scalar=m, in1=t,
                                                op0=alu.mult, op1=alu.add), "sV")
            for g in range(G):
                for b in range(B):
                    p.do("vector", lambda E, o=tp[b][:],
                         a=xn[b][:, g * (CH + 1):g * (CH + 1) + CH], m1=vec["muk1"][:, g:g + 1]:
                         E.tensor_scalar(out=o, in0=a, scalar1=m1, scalar2=None, op0=alu.mult), "sV")
                for b in range(B):
                    p.do("vector", lambda E, o=mixk[b][:, g * CH:(g + 1) * CH],
                         a=xn[b][:, g * (CH + 1) + 1:(g + 1) * (CH + 1)],
                         m=vec["muk"][:, g:g + 1], t=tp[b][:]:
                         E.scalar_tensor_tensor(out=o, in0=a, scalar=m, in1=t,
                                                op0=alu.mult, op1=alu.add), "sV")
            mixv = p.cur("sV")
            p.wait("tensor", wv[2])
            p.wait("tensor", "sV", mixv)
            p.wait("scalar", "sV", prev_ov)
            for m in range(G):
                relus = []
                for b in range(B):
                    i, bank = banks.acquire()
                    mm_group(bank, wslices(G * D, m, G, D), rslices(mixk[b], G))
                    banks.wrote(i)
                    banks.pre_read(i, "scalar")
                    tslot = gt[(m % 2) * 2 + b]
                    v = p.do("scalar", lambda E, o=tslot[:], s=bank:
                             E.activation(out=o, in_=s[:], func=AF.Relu, bias=0.0,
                                          scale=1.0), "sA")
                    banks.post_read(i, "sA", v)
                    relus.append(tslot)
                for b in range(B):
                    p.do("scalar", lambda E, o=kact[b][:, m * CH:(m + 1) * CH], a=relus[b][:]:
                         E.activation(out=o, in_=a, func=AF.Square, bias=0.0, scale=1.0), "sA")
            kvv = p.cur("sA")
            p.wait("tensor", "sA", kvv)
            for m in range(G):
                gates = []
                for b in range(B):
                    i, bank = banks.acquire()
                    mm_group(bank, wslices(0, m, G, D), rslices(mixr[b], G))
                    banks.wrote(i)
                    banks.pre_read(i, "scalar")
                    gslot = gt[(m % 2) * 2 + b]
                    v = p.do("scalar", lambda E, o=gslot[:], s=bank:
                             E.activation(out=o, in_=s[:], func=AF.Sigmoid, bias=0.0,
                                          scale=1.0), "sA")
                    banks.post_read(i, "sA", v)
                    gates.append((gslot, v))
                hb = []
                for b in range(B):
                    i, bank = banks.acquire()
                    mm_group(bank, wslices(2 * G * D, m, G, D), rslices(kact[b], G))
                    banks.wrote(i)
                    hb.append((i, bank))
                for b in range(B):
                    i, bank = hb[b]
                    banks.pre_read(i, "vector")
                    p.wait("vector", "sA", gates[b][1])
                    p.do("vector", lambda E, o=tp[b][:], s=bank, r_=gates[b][0][:]:
                         E.tensor_tensor(out=o, in0=s[:], in1=r_, op=alu.mult), None)
                for b in range(B):
                    i, bank = hb[b]
                    v = p.do("vector", lambda E, o=xch[b][:, m * CH:(m + 1) * CH],
                             t=tp[b][:], xx=xch[b][:, m * CH:(m + 1) * CH]:
                             E.scalar_tensor_tensor(out=o, in0=t, scalar=0.5, in1=xx,
                                                    op0=alu.mult, op1=alu.add), "sV")
                    banks.post_read(i, "sV", v)
            ov = p.cur("sV")
            p.wait("sync", "sV", ov)
            prev_st = [dma_store(xslice(dst_dram, b, c0), xch[b][:], f"ox{b}")
                       for b in range(B)]
            prev_ov = ov
            prev_mm = p.cur("sT")

    def phase_l0attn_rkv(src_dram):
        wv = [load_weight(0, WrT, D, D), load_weight(G * D, WkT, D, D),
              load_weight(2 * G * D, WvT, D, D)]
        prev_xv = p.cur("sV")
        prev_mm = p.cur("sT")
        slot_st = [None] * 4
        for c in range(NCH):
            c0 = c * CH
            p.wait("sync", "sV", prev_xv)
            lv = [dma_load(xch[b][:], xslice(src_dram, b, c0), f"x{b}") for b in range(B)]
            p.wait("scalar", "sT", prev_mm)
            cvals, svals = cast_and_square(xch, lv)
            bval = ln_stats(xbf, xsq, G, cvals, svals)
            xnv = build_xn(xch, bval)
            prev_xv = xnv
            p.wait("tensor", "sV", xnv)
            p.wait("tensor", wv[2])
            xnsl = [[xn[b][:, g * (CH + 1) + 1:(g + 1) * (CH + 1)] for g in range(G)]
                    for b in range(B)]
            si = 0
            for j in range(3):
                for m in range(G):
                    for b in range(B):
                        i, bank = banks.acquire()
                        mm_group(bank, wslices(j * G * D, m, G, D), xnsl[b])
                        banks.wrote(i)
                        banks.pre_read(i, "scalar")
                        slot = si % 4
                        si += 1
                        if slot_st[slot]:
                            p.wait("scalar", slot_st[slot])
                        v = p.do("scalar", lambda E, o=gtb[slot], s=bank:
                                 E.activation(out=o, in_=s[:], func=AF.Copy, bias=0.0,
                                              scale=0.5), "sA")
                        banks.post_read(i, "sA", v)
                        p.wait("sync", "sA", v)
                        slot_st[slot] = dma_store(rkv_s[b, j, m, :, c0:c0 + CH],
                                                  gtb[slot], f"st{slot}")
            prev_mm = p.cur("sT")

    def phase_mixer():
        NLC = T // LC
        p.do("gpsimd", lambda E: E.memset(lifst[:], 0.0), None)
        p.do("gpsimd", lambda E: E.memset(c09[:], 0.9), "sG")
        p.wait("vector", "sG", p.cur("sG"))
        wbigf = wbig.bitcast(F32)
        rin = [[F[0].bitcast(BF16)[:, 0:24 * LC], F[1].bitcast(BF16)[:, 0:24 * LC]],
               [wbig[:, 0:24 * LC], wbig[:, 24 * LC:48 * LC]]]
        vprA = F[2][:, 0:24 * LC]
        vprB = wbigf[:, 6144:6144 + 24 * LC]
        spk = [Bf[0][:, 0:24 * LC], Bf[1][:, 0:24 * LC]]
        hH = [xn[0].bitcast(F32)[:, 0:2 * 8 * LC], xn[1].bitcast(F32)[:, 0:2 * 8 * LC]]
        prev_sg = p.cur("sG")
        prev_sv = p.cur("sV")
        ydma = {0: None, 1: None}
        for lc in range(NLC):
            t0 = lc * LC
            ring = lc % 2
            p.wait("sync", "sV", prev_sv if lc >= 2 else 0)
            lvs = []
            for b in range(B):
                lvs.append(dma_load(rin[ring][b][:],
                                    rkv_s[b, :, :, :, t0:t0 + LC].transpose([2, 0, 1, 3]),
                                    f"r{ring}{b}"))
            p.wait("vector", lvs[0])
            p.wait("vector", lvs[1])
            p.wait("vector", "sG", prev_sg)
            for t in range(LC):
                p.do("vector", lambda E, o=vprA[:, t::LC], s=lifst[:, 0:24],
                     xx=rin[ring][0][:, t::LC]:
                     E.scalar_tensor_tensor(out=o, in0=s, scalar=0.5, in1=xx,
                                            op0=alu.mult, op1=alu.add), None)
                p.do("vector", lambda E, o=vprB[:, t::LC], s=lifst[:, 24:48],
                     xx=rin[ring][1][:, t::LC]:
                     E.scalar_tensor_tensor(out=o, in0=s, scalar=0.5, in1=xx,
                                            op0=alu.mult, op1=alu.add), None)
                p.do("vector", lambda E, o=lifst[:, 0:24], a=vprA[:, t::LC]:
                     E.scalar_tensor_tensor(out=o, in0=a, scalar=1.0, in1=a,
                                            op0=alu.is_lt, op1=alu.mult), None)
                p.do("vector", lambda E, o=lifst[:, 24:48], a=vprB[:, t::LC]:
                     E.scalar_tensor_tensor(out=o, in0=a, scalar=1.0, in1=a,
                                            op0=alu.is_lt, op1=alu.mult), None)
            lifv = p.do("vector", lambda E: E.engine_nop(), "sV")
            p.wait("gpsimd", "sV", lifv)
            for b in range(B):
                vp = vprA if b == 0 else vprB
                for j in range(3):
                    p.do("gpsimd", lambda E, o=spk[b][:, j * 8 * LC:(j + 1) * 8 * LC],
                         a=vp[:, j * 8 * LC:(j + 1) * 8 * LC]:
                         E.tensor_single_scalar(out=o, in_=a, scalar=1.0, op=alu.is_ge), None)
            for b in range(B):
                p.do("gpsimd", lambda E, o=kvb[:, b * 8 * LC:(b + 1) * 8 * LC],
                     a=spk[b][:, 8 * LC:16 * LC], bb=spk[b][:, 16 * LC:24 * LC]:
                     E.tensor_tensor(out=o, in0=a, in1=bb, op=alu.mult), None)
            kvg = p.do("gpsimd", lambda E: E.engine_nop(), "sG")
            prev_sg = kvg
            p.wait("vector", "sG", kvg)
            hv0 = p.cur("sV")
            for g in range(G):
                for b in range(B):
                    hr = hH[b][:, (ring * 8 + g) * LC:(ring * 8 + g + 1) * LC]
                    if lc == 0:
                        ini = 0.0
                    else:
                        pr = hH[b][:, ((1 - ring) * 8 + g) * LC + LC - 1:
                                   ((1 - ring) * 8 + g) * LC + LC]
                        ini = pr
                    p.do("vector", lambda E, o=hr, d0=c09[:].broadcast_to([128, LC]),
                         d1=kvb[:, (b * 8 + g) * LC:(b * 8 + g + 1) * LC], i_=ini:
                         E.tensor_tensor_scan(out=o, data0=d0, data1=d1, initial=i_,
                                              op0=alu.mult, op1=alu.add), None)
            hv = p.do("vector", lambda E: E.engine_nop(), "sV")
            prev_sv = hv
            p.wait("gpsimd", "sV", hv)
            if ydma[ring]:
                p.wait("gpsimd", ydma[ring])
            for b in range(B):
                p.do("gpsimd", lambda E, o=ybuf[:, (ring * 16 + b * 8) * LC:
                                                (ring * 16 + b * 8 + 8) * LC],
                     a=spk[b][:, 0:8 * LC],
                     h_=hH[b][:, ring * 8 * LC:(ring + 1) * 8 * LC]:
                     E.tensor_tensor(out=o, in0=a, in1=h_, op=alu.mult), None)
            yg = p.do("gpsimd", lambda E: E.engine_nop(), "sG")
            p.wait("sync", "sG", yg)
            ydma[ring] = dma_store(y_s[:, :, :, t0:t0 + LC].transpose([2, 0, 1, 3]),
                                   ybuf[:, ring * 16 * LC:(ring + 1) * 16 * LC],
                                   f"y{ring}")

    def phase_x1(src_dram):
        wv = load_weight(0, WoT, D, D)
        prev_mm = p.cur("sT")
        prev_sv = p.cur("sV")
        prev_st = []
        ybig = [Bf[0], Bf[1]]
        xbf2 = [Bf[2], Bf[3]]
        xsq2 = [kact_ar[:, 0:G * CH], kact_ar[:, G * CH:2 * G * CH]]
        for c in range(NCH):
            c0 = c * CH
            p.wait("sync", "sV", prev_sv)
            for tok in prev_st:
                p.wait("sync", tok)
            lv = [dma_load(xch[b][:], xslice(src_dram, b, c0), f"x{b}") for b in range(B)]
            lvy = [dma_load(ybig[b][:], xslice(y_s, b, c0), f"a{b}") for b in range(B)]
            cvals, svals = [], []
            for b in range(B):
                p.wait("scalar", lv[b])
                p.wait("scalar", "sT", prev_mm)
                cv = p.do("scalar", lambda E, o=xbf2[b][:], a=xch[b][:]:
                          E.activation(out=o, in_=a, func=AF.Copy, bias=0.0, scale=1.0), "sA")
                cvals.append(cv)
            for b in range(B):
                sv = p.do("scalar", lambda E, o=xsq2[b][:], a=xch[b][:]:
                          E.activation(out=o, in_=a, func=AF.Square, bias=0.0, scale=1.0), "sA")
                svals.append(sv)
            bval = ln_stats(xbf2, xsq2, G, cvals, svals)
            p.wait("tensor", wv)
            for tok in lvy:
                p.wait("tensor", tok)
            p.wait("vector", "sA", bval)
            for m in range(G):
                hb = []
                for b in range(B):
                    i, bank = banks.acquire()
                    mm_group(bank, wslices(0, m, G, D), rslices(ybig[b], G))
                    banks.wrote(i)
                    hb.append((i, bank))
                for b in range(B):
                    p.do("vector", lambda E, o=tp[b][:],
                         a=xch[b][:, m * CH:(m + 1) * CH], r_=rbc[b][:]:
                         E.tensor_tensor(out=o, in0=a, in1=r_, op=alu.mult), None)
                for b in range(B):
                    p.do("vector", lambda E, o=tp[b][:], a=tp[b][:], m_=rmubc[b][:]:
                         E.tensor_tensor(out=o, in0=a, in1=m_, op=alu.subtract), None)
                for b in range(B):
                    i, bank = hb[b]
                    banks.pre_read(i, "vector")
                    v = p.do("vector", lambda E, o=xch[b][:, m * CH:(m + 1) * CH],
                             s=bank, t=tp[b][:]:
                             E.tensor_tensor(out=o, in0=s[:], in1=t, op=alu.add), "sV")
                    banks.post_read(i, "sV", v)
            x1pv = p.cur("sV")
            cv2, sv2 = [], []
            p.wait("scalar", "sV", x1pv)
            p.wait("scalar", "sT", p.cur("sT"))
            for b in range(B):
                cv = p.do("scalar", lambda E, o=xbf2[b][:], a=xch[b][:]:
                          E.activation(out=o, in_=a, func=AF.Copy, bias=0.0, scale=1.0), "sA")
                cv2.append(cv)
            for b in range(B):
                sv = p.do("scalar", lambda E, o=xsq2[b][:], a=xch[b][:]:
                          E.activation(out=o, in_=a, func=AF.Square, bias=0.0, scale=1.0), "sA")
                sv2.append(sv)
            bval2 = ln_stats(xbf2, xsq2, G, cv2, sv2)
            p.wait("vector", "sA", bval2)
            for m in range(G):
                for b in range(B):
                    p.do("vector", lambda E, o=tp[b][:],
                         a=xch[b][:, m * CH:(m + 1) * CH], r_=rbc[b][:]:
                         E.tensor_tensor(out=o, in0=a, in1=r_, op=alu.mult), None)
                for b in range(B):
                    p.do("vector", lambda E, o=gt[b][:], a=tp[b][:], m_=rmubc[b][:]:
                         E.tensor_tensor(out=o, in0=a, in1=m_, op=alu.subtract), None)
                for b in range(B):
                    v = p.do("vector", lambda E, o=xch[b][:, m * CH:(m + 1) * CH],
                             a=gt[b][:], g1=vec["ln1g"][:, m:m + 1], bb1=vec["ln1b"][:, m:m + 1]:
                             E.tensor_scalar(out=o, in0=a, scalar1=g1, scalar2=bb1,
                                             op0=alu.mult, op1=alu.add), "sV")
            ov = p.cur("sV")
            p.wait("sync", "sV", ov)
            prev_st = [dma_store(xslice(x1_s, b, c0), xch[b][:], f"ox{b}")
                       for b in range(B)]
            prev_mm = p.cur("sT")
            prev_sv = ov

    def phase_w1():
        prev_mm = p.cur("sT")
        slot_st = [None] * 4
        slot_mm = {}
        for c in range(NCH):
            c0 = c * CH
            p.wait("sync", "sA", p.cur("sA") if c else 0)
            lv = [dma_load(xch[b][:], xslice(x1_s, b, c0), f"x{b}") for b in range(B)]
            cvals = []
            for b in range(B):
                p.wait("scalar", lv[b])
                p.wait("scalar", "sT", prev_mm)
                cv = p.do("scalar", lambda E, o=xbf[b][:], a=xch[b][:]:
                          E.activation(out=o, in_=a, func=AF.Copy, bias=0.0, scale=1.0), "sA")
                cvals.append(cv)
            for mg in range(4):
                slot = mg % 2
                p.wait("gpsimd", "sT", slot_mm.get(slot, 0))
                ap = W1T.rearrange("(k p) o -> k p o", p=128)[:, :, mg * D:(mg + 1) * D]
                ap = ap.transpose([1, 0, 2])
                pv = dma_cast(wbig[:, slot * G * D:slot * G * D + G * D], ap, f"w{slot}")
                p.wait("tensor", pv)
                p.wait("tensor", "sA", max(cvals))
                for m in range(G):
                    mglob = mg * G + m
                    for b in range(B):
                        i, bank = banks.acquire()
                        mm_group(bank, wslices(slot * G * D, m, G, D),
                                 rslices(xbf[b], G),
                                 extra=(b1h[0:1, mglob * 128:(mglob + 1) * 128],
                                        onesrow[0:1, 0:CH]))
                        banks.wrote(i)
                        banks.pre_read(i, "scalar")
                        st_ = (mglob * B + b) % 4
                        if slot_st[st_]:
                            p.wait("scalar", slot_st[st_])
                        v = p.do("scalar", lambda E, o=gtb[st_], s=bank:
                                 E.activation(out=o, in_=s[:], func=AF.Copy, bias=0.0,
                                              scale=0.5), "sA")
                        banks.post_read(i, "sA", v)
                        p.wait("sync", "sA", v)
                        slot_st[st_] = dma_store(h1_s[b, mglob, :, c0:c0 + CH],
                                                 gtb[st_], f"st{st_}")
                slot_mm[slot] = p.cur("sT")
            prev_mm = p.cur("sT")

    def phase_lif(src_dram, dst_dram, ngrp, lc):
        nlc = T // lc
        W = ngrp
        p.do("gpsimd", lambda E: E.memset(lifst[:], 0.0), "sG")
        p.wait("vector", "sG", p.cur("sG"))
        nf32 = W * lc
        wbigf = wbig.bitcast(F32)
        rin = [[F[0].bitcast(BF16)[:, 0:nf32], F[1].bitcast(BF16)[:, 0:nf32]],
               [F[2].bitcast(BF16)[:, 0:nf32], wbig[:, 0:nf32]]]
        vpr = [wbigf[:, 4096:4096 + nf32], wbigf[:, 4096 + nf32:4096 + 2 * nf32]]
        spk = [Bf[0].bitcast(BF16)[:, 0:nf32], Bf[1][:, 0:nf32]]
        prev_sg = 0
        sdma = {0: None, 1: None}
        for lcx in range(nlc):
            t0 = lcx * lc
            ring = lcx % 2
            lvs = []
            p.wait("sync", "sV", p.cur("sV") if lcx >= 2 else 0)
            for b in range(B):
                lvs.append(dma_load(rin[ring][b][:],
                                    src_dram[b, :, :, t0:t0 + lc].transpose([1, 0, 2]),
                                    f"r{ring}{b}"))
            p.wait("vector", lvs[0])
            p.wait("vector", lvs[1])
            p.wait("vector", "sG", prev_sg)
            for t in range(lc):
                p.do("vector", lambda E, o=vpr[0][:, t::lc], s=lifst[:, 0:W],
                     xx=rin[ring][0][:, t::lc]:
                     E.scalar_tensor_tensor(out=o, in0=s, scalar=0.5, in1=xx,
                                            op0=alu.mult, op1=alu.add), None)
                p.do("vector", lambda E, o=vpr[1][:, t::lc], s=lifst[:, W:2 * W],
                     xx=rin[ring][1][:, t::lc]:
                     E.scalar_tensor_tensor(out=o, in0=s, scalar=0.5, in1=xx,
                                            op0=alu.mult, op1=alu.add), None)
                p.do("vector", lambda E, o=lifst[:, 0:W], a=vpr[0][:, t::lc]:
                     E.scalar_tensor_tensor(out=o, in0=a, scalar=1.0, in1=a,
                                            op0=alu.is_lt, op1=alu.mult), None)
                p.do("vector", lambda E, o=lifst[:, W:2 * W], a=vpr[1][:, t::lc]:
                     E.scalar_tensor_tensor(out=o, in0=a, scalar=1.0, in1=a,
                                            op0=alu.is_lt, op1=alu.mult), None)
            lifv = p.do("vector", lambda E: E.engine_nop(), "sV")
            p.wait("gpsimd", "sV", lifv)
            if sdma[ring]:
                p.wait("gpsimd", sdma[ring])
            for b in range(B):
                p.do("gpsimd", lambda E, o=spk[b][:], a=vpr[b][:]:
                     E.tensor_single_scalar(out=o, in_=a, scalar=1.0, op=alu.is_ge), None)
            sg = p.do("gpsimd", lambda E: E.engine_nop(), "sG")
            prev_sg = sg
            p.wait("sync", "sG", sg)
            for b in range(B):
                sdma[ring] = dma_store(dst_dram[b, :, :, t0:t0 + lc].transpose([1, 0, 2]),
                                       spk[b][:], f"y{ring}")

    def phase_w2():
        CW = 256
        wbigf = wbig
        w2a = load_weight(0, W2T[0:24 * 128, :], 24 * 128, D)
        f0b = F[0].bitcast(BF16)
        f1b = F[1].bitcast(BF16)
        ap = W2T[24 * 128:, :].rearrange("(k p) o -> k p o", p=128).transpose([1, 0, 2])
        dma_cast(f0b[:, 0:4 * D], ap[:, 0:4, :])
        w2c = dma_cast(f1b[:, 0:4 * D], ap[:, 4:8, :])

        def w2slices(m):
            res = [wbig[:, g * D + m * 128:g * D + m * 128 + 128] for g in range(24)]
            res += [f0b[:, g * D + m * 128:g * D + m * 128 + 128] for g in range(4)]
            res += [f1b[:, g * D + m * 128:g * D + m * 128 + 128] for g in range(4)]
            return res

        def s1sl(b, g):
            buf = Bf[2 * b + (g // 16)]
            gg = g % 16
            return buf[:, gg * CW:(gg + 1) * CW]
        slot_st = [None] * 4
        prev_mm = p.cur("sT")
        for c in range(T // CW):
            c0 = c * CW
            p.wait("sync", "sT", prev_mm)
            lv = []
            for b in range(B):
                for half in range(2):
                    lv.append(dma_load(
                        Bf[2 * b + half][:],
                        s1_s[b, half * 16:(half + 1) * 16, :, c0:c0 + CW]
                        .transpose([1, 0, 2]), f"a{b}"))
            p.wait("tensor", w2c)
            for tok in lv:
                p.wait("tensor", tok)
            for m in range(G):
                for b in range(B):
                    i, bank = banks.acquire()
                    sl = [s1sl(b, g) for g in range(GF)]
                    mm_group(bank[:, 0:CW], w2slices(m), sl,
                             extra=(b2h[0:1, m * 128:(m + 1) * 128], onesrow[0:1, 0:CW]))
                    banks.wrote(i)
                    banks.pre_read(i, "scalar")
                    slot = (m * B + b) % 4
                    if slot_st[slot]:
                        p.wait("scalar", slot_st[slot])
                    v = p.do("scalar", lambda E, o=gtb[slot][:, 0:CW], s=bank:
                             E.activation(out=o, in_=s[:, 0:CW], func=AF.Copy, bias=0.0,
                                          scale=0.5), "sA")
                    banks.post_read(i, "sA", v)
                    p.wait("sync", "sA", v)
                    slot_st[slot] = dma_store(h2_s[b, m, :, c0:c0 + CW],
                                              gtb[slot][:, 0:CW], f"st{slot}")
            prev_mm = p.cur("sT")

    def phase_ln2_resid(src_dram, dst_dram):
        xbf2 = [Bf[2], Bf[3]]
        xsq2 = [kact_ar[:, 0:G * CH], kact_ar[:, G * CH:2 * G * CH]]
        ffnb = [Bf[0], Bf[1]]
        prev_sv = p.cur("sV")
        prev_st = []
        prev_mm = p.cur("sT")
        for c in range(NCH):
            c0 = c * CH
            p.wait("sync", "sV", prev_sv)
            for tok in prev_st:
                p.wait("sync", tok)
            lv = [dma_load(xch[b][:], xslice(x1_s, b, c0), f"x{b}") for b in range(B)]
            lvf = [dma_load(ffnb[b][:], xslice(ffn_s, b, c0), f"a{b}") for b in range(B)]
            for tok in lv + lvf:
                p.wait("vector", tok)
            for b in range(B):
                p.do("vector", lambda E, o=xch[b][:], a=xch[b][:], f_=ffnb[b][:]:
                     E.tensor_tensor(out=o, in0=a, in1=f_, op=alu.add), "sV")
            xpv = p.cur("sV")
            cvals, svals = [], []
            p.wait("scalar", "sV", xpv)
            p.wait("scalar", "sT", prev_mm)
            for b in range(B):
                cv = p.do("scalar", lambda E, o=xbf2[b][:], a=xch[b][:]:
                          E.activation(out=o, in_=a, func=AF.Copy, bias=0.0, scale=1.0), "sA")
                cvals.append(cv)
            for b in range(B):
                sv = p.do("scalar", lambda E, o=xsq2[b][:], a=xch[b][:]:
                          E.activation(out=o, in_=a, func=AF.Square, bias=0.0, scale=1.0), "sA")
                svals.append(sv)
            bval = ln_stats(xbf2, xsq2, G, cvals, svals)
            p.wait("vector", "sA", bval)
            for m in range(G):
                for b in range(B):
                    p.do("vector", lambda E, o=tp[b][:],
                         a=xch[b][:, m * CH:(m + 1) * CH], r_=rbc[b][:]:
                         E.tensor_tensor(out=o, in0=a, in1=r_, op=alu.mult), None)
                for b in range(B):
                    p.do("vector", lambda E, o=gt[b][:], a=tp[b][:], m_=rmubc[b][:]:
                         E.tensor_tensor(out=o, in0=a, in1=m_, op=alu.subtract), None)
                for b in range(B):
                    p.do("vector", lambda E, o=xch[b][:, m * CH:(m + 1) * CH],
                         a=gt[b][:], g2=vec["ln2g"][:, m:m + 1], bb2=vec["ln2b"][:, m:m + 1]:
                         E.tensor_scalar(out=o, in0=a, scalar1=g2, scalar2=bb2,
                                         op0=alu.mult, op1=alu.add), "sV")
            p.wait("sync", "sT", p.cur("sT"))
            prev_st = []
            for b in range(B):
                lx = dma_load(F[2][:], xslice(src_dram, b, c0), "x2")
                p.wait("vector", lx)
                v = p.do("vector", lambda E, o=xch[b][:], a=xch[b][:], xx=F[2][:]:
                         E.tensor_tensor(out=o, in0=a, in1=xx, op=alu.add), "sV")
                p.wait("sync", "sV", v)
                prev_st.append(dma_store(xslice(dst_dram, b, c0), xch[b][:], f"ox{b}"))
            prev_sv = p.cur("sV")
            prev_mm = p.cur("sT")

    def phase_l1ffn_quant():
        qb = Banks(p, ps[0:4])
        ws1ps = stps
        NW = float(D * 4 * D)
        p.do("gpsimd", lambda E: E.memset(cbacc[:], 0.0), "sG")
        p.wait("vector", "sG", p.cur("sG"))
        for w_i, (wsrc, npiece, pw) in enumerate(((WfcT, 8, 4 * D), (WprojT, 32, D))):
            for k in range(npiece):
                buf = F[k % 2][:, 0:pw]
                p.wait("sync", "sV", p.cur("sV") if k >= 2 else 0)
                lv = dma_load(buf, wsrc[k * 128:(k + 1) * 128, :], f"x{k % 2}")
                p.wait("vector", lv)
                col = (0 if w_i == 0 else 32) + k
                p.do("vector", lambda E, o=lifst[:, col:col + 1], a=buf:
                     E.tensor_reduce(out=o, in_=a, axis=mybir.AxisListType.X,
                                     op=alu.add, apply_absolute_value=True), "sV")
            off = 0 if w_i == 0 else 32
            p.barrier()
            p.do("vector", lambda E, o=lifst[:, 30 + w_i:31 + w_i],
                 a=lifst[:, off:off + npiece]:
                 E.tensor_reduce(out=o, in_=a, axis=mybir.AxisListType.X,
                                 op=alu.add), "sV")
            rv = p.cur("sV")
            i, bank = qb.acquire()
            p.wait("tensor", "sV", rv)
            p.do("tensor", lambda E, bk=bank, w=lifst[:, 30 + w_i:31 + w_i], r=onescolf[:]:
                 E.matmul(bk[0:1, 0:1], w, r, start=True, stop=True), "sT")
            qb.wrote(i)
            qb.pre_read(i, "vector")
            p.do("vector", lambda E, o=throw[0:1, w_i:w_i + 1], s=bank:
                 E.tensor_scalar(out=o, in0=s[0:1, 0:1], scalar1=1.0 / NW,
                                 scalar2=1e-6, op0=alu.mult, op1=alu.add), "sV")
            p.do("vector", lambda E, o=throw[0:1, 2 + w_i:3 + w_i], s=bank:
                 E.tensor_scalar(out=o, in0=s[0:1, 0:1], scalar1=0.5 / NW,
                                 scalar2=0.5e-6, op0=alu.mult, op1=alu.add), "sV")
            v = p.do("vector", lambda E, o=throw[0:1, 4 + w_i:5 + w_i], s=bank:
                     E.tensor_scalar(out=o, in0=s[0:1, 0:1], scalar1=-0.5 / NW,
                                     scalar2=-0.5e-6, op0=alu.mult, op1=alu.add), "sV")
            qb.post_read(i, "sV", v)
        p.barrier()
        thv = p.cur("sV")
        p.wait("tensor", "sV", thv)
        for row_i, dst_i in ((2, 0), (4, 1), (3, 2), (5, 3)):
            i, bank = qb.acquire()
            p.do("tensor", lambda E, bk=bank, r=throw[0:1, row_i:row_i + 1]:
                 E.matmul(bk[:, 0:1], ones1f[:], r, start=True, stop=True), "sT")
            qb.wrote(i)
            qb.pre_read(i, "scalar")
            v = p.do("scalar", lambda E, o=thcol[:, dst_i:dst_i + 1], s=bank:
                     E.activation(out=o, in_=s[:, 0:1], func=AF.Copy, bias=0.0,
                                  scale=1.0), "sA")
            qb.post_read(i, "sA", v)
        p.barrier()
        thav = p.cur("sA")
        p.wait("vector", "sA", thav)
        for w_i, (wsrc, wdst, npiece, pw, fold) in enumerate(
                ((WfcT, wfcq_s, 8, 4 * D, False), (WprojT, wprq_s, 32, D, True))):
            tcp, tcn = (0, 1) if w_i == 0 else (2, 3)
            sqtok = []
            for k0 in range(0, npiece, 2):
                lvv = []
                for kk in range(2):
                    k = k0 + kk
                    p.wait("sync", "sV", p.cur("sV") if k0 >= 2 else 0)
                    p.wait("sync", "sT", p.cur("sT") if k0 >= 2 else 0)
                    lvv.append(dma_load(F[kk][:, 0:pw],
                                        wsrc[k * 128:(k + 1) * 128, :], f"x{kk}"))
                for tok in lvv:
                    p.wait("vector", tok)
                p.wait("vector", "sT", p.cur("sT") if k0 >= 2 else 0)
                for tok in sqtok:
                    p.wait("vector", tok)
                q1 = [Bf[0].bitcast(BF16)[:, 0:pw], Bf[1].bitcast(BF16)[:, 0:pw]]
                q2 = [Bf[2].bitcast(BF16)[:, 0:pw], Bf[3].bitcast(BF16)[:, 0:pw]]
                qo = [kact_ar[:, 0:pw], kact_ar[:, 4 * D:4 * D + pw]]
                for kk in range(2):
                    p.do("vector", lambda E, o=q1[kk], a=F[kk][:, 0:pw],
                         t=thcol[:, tcp:tcp + 1]:
                         E.tensor_scalar(out=o, in0=a, scalar1=t, scalar2=None,
                                         op0=alu.is_gt), None)
                for kk in range(2):
                    p.do("vector", lambda E, o=q2[kk], a=F[kk][:, 0:pw],
                         t=thcol[:, tcn:tcn + 1]:
                         E.tensor_scalar(out=o, in0=a, scalar1=t, scalar2=None,
                                         op0=alu.is_lt), None)
                for kk in range(2):
                    p.do("vector", lambda E, o=qo[kk], a=q1[kk], bb=q2[kk]:
                         E.tensor_tensor(out=o, in0=a, in1=bb, op=alu.subtract),
                         "sV" if kk == 1 else None)
                qv = p.cur("sV")
                if fold:
                    p.wait("tensor", "sV", qv)
                    for kk in range(2):
                        k = k0 + kk
                        for m in range(G):
                            i, bank = qb.acquire()
                            p.do("tensor", lambda E, bk=bank,
                                 w=qo[kk][:, m * 128:(m + 1) * 128],
                                 r=lnb_sb[:, k:k + 1]:
                                 E.matmul(bk[:, 0:1], w, r, start=True, stop=True), "sT")
                            qb.wrote(i)
                            qb.pre_read(i, "vector")
                            v = p.do("vector", lambda E, o=cbacc[:, m:m + 1], s=bank:
                                     E.tensor_tensor(out=o, in0=o, in1=s[:, 0:1],
                                                     op=alu.add), "sV")
                            qb.post_read(i, "sV", v)
                    cbmm = p.cur("sT")
                    p.wait("scalar", "sT", cbmm)
                    for kk in range(2):
                        k = k0 + kk
                        p.do("scalar", lambda E, o=qo[kk], a=qo[kk], s=lnw_sb[:, k:k + 1]:
                             E.activation(out=o, in_=a, func=AF.Copy, bias=0.0,
                                          scale=s), "sA")
                    fav = p.cur("sA")
                    p.wait("tensor", "sA", fav)
                    for kk in range(2):
                        k = k0 + kk
                        for half in range(2):
                            p.do("tensor", lambda E, bk=ws1ps[half], w=ones1b,
                                 r=qo[kk][:, half * CH:(half + 1) * CH],
                                 st=(k == 0), sp=(k == npiece - 1):
                                 E.matmul(bk[:], w[:], r, start=st, stop=sp),
                                 "sT" if half == 1 else None)
                    p.wait("sync", "sA", fav)
                    p.wait("sync", "sT", p.cur("sT"))
                else:
                    p.wait("sync", "sV", qv)
                sqtok = [dma_store(wdst[(k0 + kk) * 128:(k0 + kk + 1) * 128, :],
                                   qo[kk], f"sq{kk}") for kk in range(2)]
        p.barrier()
        dma_store(qrows[:], rowar[:], "dbgq")
        dma_store(qthcol[:], thcol[:], "dbgq")
        dma_store(qlifst[:], lifst[:], "dbgq")
        p.barrier()
        p.wait("scalar", "sV", p.cur("sV"))
        p.wait("scalar", "sT", p.cur("sT"))
        p.do("scalar", lambda E, o=cbh[:], a=cbacc:
             E.activation(out=o, in_=a[:], func=AF.Copy, bias=0.0, scale=0.5), "sA")
        p.wait("vector", "sT", p.cur("sT"))
        for half in range(2):
            p.do("vector", lambda E, o=ws1neg[0:1, half * CH:(half + 1) * CH],
                 s=ws1ps[half]:
                 E.tensor_scalar(out=o, in0=s[0:1, 0:CH], scalar1=-1.0, scalar2=None,
                                 op0=alu.mult), "sV")

    def phase_l1ffn(src_dram, dst_dram):
        bk4 = Banks(p, ps[0:4])
        wl = []
        for k in range(8):
            dst = wbig[:, k * 4 * D:(k + 1) * 4 * D] if k < 6 else                 kact_ar[:, (k - 6) * 4 * D:(k - 5) * 4 * D]
            wl.append(dma_load(dst, wfcq_s[k * 128:(k + 1) * 128, :], "lw"))

        def wfcsl(mglob):
            res = []
            for k in range(8):
                if k < 6:
                    res.append(wbig[:, k * 4 * D + mglob * 128:k * 4 * D + mglob * 128 + 128])
                else:
                    res.append(kact_ar[:, (k - 6) * 4 * D + mglob * 128:
                                       (k - 6) * 4 * D + mglob * 128 + 128])
            return res

        prev_mm = p.cur("sT")
        prev_sv = p.cur("sV")
        kslot_st = [None, None]
        for c in range(NCH):
            c0 = c * CH
            p.wait("sync", "sV", prev_sv)
            lv = [dma_load(xch[b][:], xslice(src_dram, b, c0), f"x{b}") for b in range(B)]
            p.wait("scalar", "sT", prev_mm)
            cvals, svals = [], []
            for b in range(B):
                p.wait("scalar", lv[b])
                cv = p.do("scalar", lambda E, o=xbf[b][:], a=xch[b][:]:
                          E.activation(out=o, in_=a, func=AF.Copy, bias=0.0, scale=1.0), "sA")
                cvals.append(cv)
            for b in range(B):
                sv = p.do("scalar", lambda E, o=xsq[b][:], a=xch[b][:]:
                          E.activation(out=o, in_=a, func=AF.Square, bias=0.0, scale=1.0), "sA")
                svals.append(sv)
            bval = ln_stats(xbf, xsq, G, cvals, svals, trk=bk4)
            for tok_ in kslot_st:
                if tok_:
                    p.wait("vector", tok_)
            xnv = build_xn(xch, bval)
            prev_sv = xnv
            xnsl = [[xn[b][:, g * (CH + 1) + 1:(g + 1) * (CH + 1)] for g in range(G)]
                    for b in range(B)]
            p.wait("tensor", "sV", xnv)
            p.wait("tensor", wl[-1])
            si = 0
            for mglob in range(GF):
                for b in range(B):
                    i, bank = bk4.acquire()
                    mm_group(bank, wfcsl(mglob), xnsl[b])
                    bk4.wrote(i)
                    bk4.pre_read(i, "scalar")
                    slot = si % 4
                    ks = si % 2
                    si += 1
                    p.do("scalar", lambda E, o=gt[slot][:], s=bank:
                             E.activation(out=o, in_=s[:], func=AF.Gelu, bias=0.0,
                                          scale=1.0), None)
                    if kslot_st[ks]:
                        p.wait("scalar", kslot_st[ks])
                    v = p.do("scalar", lambda E, o=kvb[:, ks * CH:(ks + 1) * CH],
                             a=gt[slot][:]:
                             E.activation(out=o, in_=a, func=AF.Copy, bias=0.0,
                                          scale=1.0), "sA")
                    bk4.post_read(i, "sA", v)
                    p.wait("sync", "sA", v)
                    kslot_st[ks] = dma_store(s1_s[b, mglob, :, c0:c0 + CH],
                                             kvb[:, ks * CH:(ks + 1) * CH], f"st{ks}")
            prev_mm = p.cur("sT")
        p.barrier()
        bk4b = Banks(p, ps[0:4])
        stb = Banks(p, stps)
        CW = 256
        wl2 = []
        for k in range(GF):
            if k < 24:
                dst = wbig[:, k * D:(k + 1) * D]
            else:
                dst = kact_ar[:, (k - 24) * D:(k - 23) * D]
            wl2.append(dma_load(dst, wprq_s[k * 128:(k + 1) * 128, :], "lw"))

        def wprsl(m):
            res = [wbig[:, k * D + m * 128:k * D + m * 128 + 128] for k in range(24)]
            res += [kact_ar[:, (k - 24) * D + m * 128:(k - 24) * D + m * 128 + 128]
                    for k in range(24, GF)]
            return res

        def hsl(b, g, cw=CW):
            buf = Bf[2 * b + (g // 16)]
            gg = g % 16
            return buf[:, gg * cw:(gg + 1) * cw]

        prev_mm = p.cur("sT")
        prev_sv2 = p.cur("sV")
        for c in range(T // CW):
            c0 = c * CW
            p.wait("sync", "sT", prev_mm)
            p.wait("sync", "sV", prev_sv2)
            lvh = []
            for b in range(B):
                for half in range(2):
                    lvh.append(dma_load(
                        Bf[2 * b + half][:, 0:16 * CW],
                        s1_s[b, half * 16:(half + 1) * 16, :, c0:c0 + CW]
                        .transpose([1, 0, 2]), f"a{b}"))
            lvx = [dma_load(F[b][:, 0:G * CW], xslice(src_dram, b, c0, CW), f"x{b}")
                   for b in range(B)]
            sums, sqs = [], []
            for tok in lvh:
                p.wait("tensor", tok)
            for b in range(B):
                i, bank = stb.acquire()
                for g in range(GF):
                    p.do("tensor", lambda E, bk=bank, w=ones1b, r=hsl(b, g),
                         st=(g == 0), sp=(g == GF - 1):
                         E.matmul(bk[0:1, 0:CW], w[:], r, start=st, stop=sp),
                         "sT" if g == GF - 1 else None)
                stb.wrote(i)
                sums.append((stb, i, bank))
            p.wait("tensor", "sV", max(banks.state[4]["rval"], banks.state[5]["rval"]))
            for b in range(B):
                for tok in lvh:
                    p.wait("scalar", tok)
                sqmm = {0: 0, 1: 0}
                for g in range(GF):
                    ks = g % 2
                    sqslot = ybuf[:, 2048 + ks * CW:2048 + (ks + 1) * CW]
                    p.wait("scalar", "sT", sqmm[ks])
                    p.do("scalar", lambda E, o=sqslot, a=hsl(b, g):
                         E.activation(out=o, in_=a, func=AF.Square, bias=0.0, scale=1.0),
                         "sA")
                    sv = p.cur("sA")
                    p.wait("tensor", "sA", sv)
                    sqmm[ks] = p.do("tensor", lambda E, bk=ps[4 + b], w=ones1b,
                                    r=sqslot,
                                    st=(g == 0), sp=(g == GF - 1):
                                    E.matmul(bk[0:1, 0:CW], w[:], r, start=st, stop=sp),
                                    "sT")
                sqs.append((banks, 4 + b, ps[4 + b]))
            banks.state[4]["wval"] = p.cur("sT")
            banks.state[5]["wval"] = p.cur("sT")
            bval = ln_finish(sums, sqs, 1.0 / (4 * D), width=CW, trk=bk4b)
            p.wait("tensor", "sA", bval)
            p.wait("tensor", wl2[-1])
            p.wait("vector", "sA", bval)
            for m in range(G):
                hb = []
                for b in range(B):
                    i, bank = bk4b.acquire()
                    mm_group(bank[:, 0:CW], wprsl(m), [hsl(b, g) for g in range(GF)],
                             extra=(ws1neg[0:1, m * 128:(m + 1) * 128],
                                    srow[b][0:1, 0:CW]))
                    bk4b.wrote(i)
                    hb.append((i, bank))
                for b in range(B):
                    i, bank = hb[b]
                    bk4b.pre_read(i, "vector")
                    p.do("vector", lambda E, o=tp[b][:, 0:CW], s=bank,
                         r_=rbc[b][:, 0:CW]:
                         E.tensor_tensor(out=o, in0=s[:, 0:CW], in1=r_, op=alu.mult), None)
                for b in range(B):
                    p.do("vector", lambda E, o=gt[b][:, 0:CW], a=tp[b][:, 0:CW],
                         cb_=cbh[:, m:m + 1]:
                         E.tensor_scalar(out=o, in0=a, scalar1=0.5, scalar2=cb_,
                                         op0=alu.mult, op1=alu.add), None)
                for b in range(B):
                    i, bank = hb[b]
                    p.wait("vector", lvx[b])
                    v = p.do("vector", lambda E, o=F[b][:, m * CW:(m + 1) * CW],
                             a=gt[b][:, 0:CW], xx=F[b][:, m * CW:(m + 1) * CW]:
                             E.tensor_tensor(out=o, in0=a, in1=xx, op=alu.add), "sV")
                    bk4b.post_read(i, "sV", v)
            ov = p.cur("sV")
            p.wait("sync", "sV", ov)
            for b in range(B):
                dma_store(xslice(dst_dram, b, c0, CW), F[b][:, 0:G * CW], f"ox{b}")
            prev_mm = p.cur("sT")
            prev_sv2 = ov

    def phase_l1attn_rkv(src_dram, half):
        bk4 = Banks(p, ps)
        wv = []
        for j in range(3):
            colbase = half * 3 * D + j * D
            ap = rkvT.rearrange("(k p) o -> k p o", p=128)[:, :, colbase:colbase + D]
            ap = ap.transpose([1, 0, 2])
            wv.append(dma_cast(wbig[:, j * G * D:(j + 1) * G * D], ap))
        prev_xv = p.cur("sV")
        prev_mm = p.cur("sT")
        slot_st = [None] * 4
        for c in range(NCH):
            c0 = c * CH
            p.wait("sync", "sV", prev_xv)
            lv = [dma_load(xch[b][:], xslice(src_dram, b, c0), f"x{b}") for b in range(B)]
            p.wait("scalar", "sT", prev_mm)
            cvals, svals = cast_and_square(xch, lv)
            bval = ln_stats(xbf, xsq, G, cvals, svals, trk=bk4)
            xnv = build_xn(xch, bval)
            prev_xv = xnv
            p.wait("tensor", "sV", xnv)
            p.wait("tensor", wv[2])
            xnsl = [[xn[b][:, g * (CH + 1) + 1:(g + 1) * (CH + 1)] for g in range(G)]
                    for b in range(B)]
            si = 0
            for j in range(3):
                for m in range(G):
                    for b in range(B):
                        i, bank = bk4.acquire()
                        mm_group(bank, wslices(j * G * D, m, G, D), xnsl[b])
                        bk4.wrote(i)
                        bk4.pre_read(i, "scalar")
                        slot = si % 4
                        si += 1
                        if slot_st[slot]:
                            p.wait("scalar", slot_st[slot])
                        fn = AF.Sigmoid if j == 0 else AF.Copy
                        v = p.do("scalar", lambda E, o=gtb[slot], s=bank, f=fn:
                                 E.activation(out=o, in_=s[:], func=f, bias=0.0,
                                              scale=1.0), "sA")
                        bk4.post_read(i, "sA", v)
                        p.wait("sync", "sA", v)
                        slot_st[slot] = dma_store(rkv_s[b, j, m, :, c0:c0 + CH],
                                                  gtb[slot], f"st{slot}")
            prev_mm = p.cur("sT")

    def phase_wkv(dst_dram, wname, uname, reverse):
        wbigf = wbig.bitcast(F32)
        xnf = [xn[0].bitcast(F32), xn[1].bitcast(F32)]
        TP1 = T + 1
        MX = [wbigf[:, 0:TP1], wbigf[:, TP1:2 * TP1]]
        NUM = [wbigf[:, 2 * TP1:3 * TP1], wbigf[:, 3 * TP1:4 * TP1]]
        DEN = [wbigf[:, 4 * TP1:5 * TP1], xnf[0][:, 0:TP1]]
        Bff = [Bf[i].bitcast(F32)[:, 0:T] for i in range(4)]
        tA = [Bff[0], Bff[1]]
        tB = [Bff[2], Bff[3]]
        kin = [F[0].bitcast(BF16)[:, 0:T], F[0].bitcast(BF16)[:, T:2 * T]]
        vin = [F[1].bitcast(BF16)[:, 0:T], F[1].bitcast(BF16)[:, T:2 * T]]
        rin_ = [F[2].bitcast(BF16)[:, 0:T], F[2].bitcast(BF16)[:, T:2 * T]]
        prev = 0
        out_tok = []
        for g in range(G):
            p.wait("sync", "sV", prev)
            lvs = []
            for b in range(B):
                lvs.append(dma_load(kin[b], rkv_s[b, 1, g, :, :], "lk"))
                lvs.append(dma_load(vin[b], rkv_s[b, 2, g, :, :], "lk"))
                lvs.append(dma_load(rin_[b], rkv_s[b, 0, g, :, :], "lk"))
            p.wait("vector", lvs[-1])
            p.wait("vector", "sA", p.cur("sA"))
            for tok in out_tok:
                p.wait("vector", tok)
            p.wait("gpsimd", "sV", prev)
            wap = vec[wname][:, g:g + 1]
            uap = vec[uname][:, g:g + 1]
            rv = lambda ap_: ap_[:, ::-1] if reverse else ap_
            bcol = T if reverse else 0
            for b in range(B):
                p.do("gpsimd", lambda E, o=MX[b][:, bcol:bcol + 1]: E.memset(o, -1e38), None)
                p.do("gpsimd", lambda E, o=NUM[b][:, bcol:bcol + 1]: E.memset(o, 0.0), None)
            p.do("gpsimd", lambda E, o=DEN[0][:, bcol:bcol + 1]: E.memset(o, 0.0), None)
            gv = p.do("gpsimd", lambda E, o=DEN[1][:, bcol:bcol + 1]: E.memset(o, 0.0), "sG")
            p.wait("vector", "sG", gv)
            if not reverse:
                post = lambda X: X[:, 1:T + 1]
                pre = lambda X: X[:, 0:T]
                ini = lambda X: X[:, 0:1]
            else:
                post = lambda X: X[:, 0:T][:, ::-1]
                pre = lambda X: X[:, 1:T + 1]
                ini = lambda X: X[:, T:T + 1]
            for b in range(B):
                p.do("vector", lambda E, o=post(MX[b]), d0=wap.broadcast_to([128, T]),
                     d1=rv(kin[b]), i_=ini(MX[b]):
                     E.tensor_tensor_scan(out=o, data0=d0, data1=d1, initial=i_,
                                          op0=alu.add, op1=alu.max), "sV")
            MXpost = [MX[b][:, 1:T + 1] if not reverse else MX[b][:, 0:T] for b in range(B)]
            MXpre_n = [MX[b][:, 0:T] if not reverse else MX[b][:, 1:T + 1] for b in range(B)]
            for b in range(B):
                p.do("vector", lambda E, o=tA[b], a=MXpre_n[b], bb=MXpost[b]:
                     E.tensor_tensor(out=o, in0=a, in1=bb, op=alu.subtract), "sV")
            for b in range(B):
                p.do("vector", lambda E, o=tB[b], a=kin[b], bb=MXpost[b]:
                     E.tensor_tensor(out=o, in0=a, in1=bb, op=alu.subtract), "sV")
            dv = p.cur("sV")
            p.wait("scalar", "sV", dv)
            for b in range(B):
                p.do("scalar", lambda E, o=tA[b], a=tA[b], w=wap:
                     E.activation(out=o, in_=a, func=AF.Exp, bias=w, scale=1.0), None)
            for b in range(B):
                p.do("scalar", lambda E, o=tB[b], a=tB[b]:
                     E.activation(out=o, in_=a, func=AF.Exp, bias=0.0, scale=1.0), "sA")
            ev = p.cur("sA")
            p.wait("vector", "sA", ev)
            for b in range(B):
                p.do("vector", lambda E, o=gtbuf[:, 0:T] if b == 0 else bcbuf[:, 0:T],
                     a=tB[b], bb=vin[b]:
                     E.tensor_tensor(out=o, in0=a, in1=bb, op=alu.mult), "sV")
            for b in range(B):
                p.do("vector", lambda E, o=post(NUM[b]), d0=rv(tA[b]),
                     d1=rv(gtbuf[:, 0:T] if b == 0 else bcbuf[:, 0:T]), i_=ini(NUM[b]):
                     E.tensor_tensor_scan(out=o, data0=d0, data1=d1, initial=i_,
                                          op0=alu.mult, op1=alu.add), "sV")
            for b in range(B):
                p.do("vector", lambda E, o=post(DEN[b]), d0=rv(tA[b]), d1=rv(tB[b]),
                     i_=ini(DEN[b]):
                     E.tensor_tensor_scan(out=o, data0=d0, data1=d1, initial=i_,
                                          op0=alu.mult, op1=alu.add), "sV")
            for b in range(B):
                p.do("vector", lambda E, o=tA[b], a=kin[b], u=uap:
                     E.tensor_scalar(out=o, in0=a, scalar1=u, scalar2=None,
                                     op0=alu.add), "sV")
            NUMpre = [NUM[b][:, 0:T] if not reverse else NUM[b][:, 1:T + 1]
                      for b in range(B)]
            DENpre = [DEN[b][:, 0:T] if not reverse else DEN[b][:, 1:T + 1]
                      for b in range(B)]
            for b in range(B):
                p.do("vector", lambda E, o=tB[b], a=MXpre_n[b], bb=tA[b]:
                     E.tensor_tensor(out=o, in0=a, in1=bb, op=alu.max), "sV")
            for b in range(B):
                p.do("vector", lambda E, o=gtbuf[:, 0:T] if b == 0 else bcbuf[:, 0:T],
                     a=MXpre_n[b], bb=tB[b]:
                     E.tensor_tensor(out=o, in0=a, in1=bb, op=alu.subtract), "sV")
            for b in range(B):
                p.do("vector", lambda E, o=tA[b], a=tA[b], bb=tB[b]:
                     E.tensor_tensor(out=o, in0=a, in1=bb, op=alu.subtract), "sV")
            e12 = p.cur("sV")
            p.wait("scalar", "sV", e12)
            for b in range(B):
                p.do("scalar", lambda E, o=gtbuf[:, 0:T] if b == 0 else bcbuf[:, 0:T],
                     a=gtbuf[:, 0:T] if b == 0 else bcbuf[:, 0:T]:
                     E.activation(out=o, in_=a, func=AF.Exp, bias=0.0, scale=1.0), None)
            for b in range(B):
                p.do("scalar", lambda E, o=tA[b], a=tA[b]:
                     E.activation(out=o, in_=a, func=AF.Exp, bias=0.0, scale=1.0), "sA")
            e12a = p.cur("sA")
            p.wait("vector", "sA", e12a)
            eb = [gtbuf[:, 0:T], bcbuf[:, 0:T]]
            for b in range(B):
                p.do("vector", lambda E, o=tB[b], a=eb[b], bb=NUMpre[b]:
                     E.tensor_tensor(out=o, in0=a, in1=bb, op=alu.mult), "sV")
            for b in range(B):
                p.do("vector", lambda E, o=eb[b], a=eb[b], bb=DENpre[b]:
                     E.tensor_tensor(out=o, in0=a, in1=bb, op=alu.mult), "sV")
            for b in range(B):
                p.do("vector", lambda E, o=eb[b], a=eb[b], bb=tA[b]:
                     E.tensor_tensor(out=o, in0=a, in1=bb, op=alu.add), "sV")
            for b in range(B):
                p.do("vector", lambda E, o=tA[b], a=tA[b], bb=vin[b]:
                     E.tensor_tensor(out=o, in0=a, in1=bb, op=alu.mult), "sV")
            for b in range(B):
                p.do("vector", lambda E, o=tB[b], a=tB[b], bb=tA[b]:
                     E.tensor_tensor(out=o, in0=a, in1=bb, op=alu.add), "sV")
            yv = p.cur("sV")
            p.wait("scalar", "sV", yv)
            for b in range(B):
                p.do("scalar", lambda E, o=eb[b], a=eb[b]:
                     E.activation(out=o, in_=a, func=AF.Ln, bias=0.0, scale=1.0), None)
            for b in range(B):
                p.do("scalar", lambda E, o=eb[b], a=eb[b]:
                     E.activation(out=o, in_=a, func=AF.Exp, bias=0.0, scale=-1.0), "sA")
            rcv = p.cur("sA")
            p.wait("vector", "sA", rcv)
            for b in range(B):
                p.do("vector", lambda E, o=tB[b], a=tB[b], bb=eb[b]:
                     E.tensor_tensor(out=o, in0=a, in1=bb, op=alu.mult), "sV")
            for b in range(B):
                p.do("vector", lambda E, o=ybuf[:, b * T:(b + 1) * T], a=tB[b], bb=rin_[b]:
                     E.tensor_tensor(out=o, in0=a, in1=bb, op=alu.mult), "sV")
            prev = p.cur("sV")
            p.wait("sync", "sV", prev)
            out_tok = [dma_store(dst_dram[b, g, :, :], ybuf[:, b * T:(b + 1) * T], "yk")
                       for b in range(B)]

    def phase_l1attn_out(src_dram, dst_dram):
        bk4 = Banks(p, ps)
        wv = load_weight(0, outwT, 2 * D, D)
        prev_sv = p.cur("sV")
        prev_st = []
        for c in range(NCH):
            c0 = c * CH
            p.wait("sync", "sV", prev_sv)
            for tok in prev_st:
                p.wait("sync", tok)
            lvf = [dma_load(Bf[b][:], xslice(y_s, b, c0), f"a{b}") for b in range(B)]
            lvb = [dma_load(Bf[2 + b][:], xslice(outb_s, b, c0), f"b{b}") for b in range(B)]
            lvx = [dma_load(xch[b][:], xslice(src_dram, b, c0), f"x{b}") for b in range(B)]
            p.wait("tensor", wv)
            for tok in lvb + lvf:
                p.wait("tensor", tok)
            for m in range(G):
                hb = []
                for b in range(B):
                    i, bank = bk4.acquire()
                    sl = [Bf[b][:, g * CH:(g + 1) * CH] for g in range(G)]
                    sl += [Bf[2 + b][:, g * CH:(g + 1) * CH] for g in range(G)]
                    mm_group(bank, wslices(0, m, 16, D), sl)
                    bk4.wrote(i)
                    hb.append((i, bank))
                for b in range(B):
                    i, bank = hb[b]
                    bk4.pre_read(i, "vector")
                    p.wait("vector", lvx[b])
                    v = p.do("vector", lambda E, o=xch[b][:, m * CH:(m + 1) * CH],
                             s=bank, xx=xch[b][:, m * CH:(m + 1) * CH]:
                             E.tensor_tensor(out=o, in0=s[:], in1=xx, op=alu.add), "sV")
                    bk4.post_read(i, "sV", v)
            ov = p.cur("sV")
            p.wait("sync", "sV", ov)
            prev_st = [dma_store(xslice(dst_dram, b, c0), xch[b][:], f"ox{b}")
                       for b in range(B)]
            prev_sv = ov

    def phase_dbg(src_dram, dst_dram, level):
        wv = [load_weight(0, fWrT, D, D), load_weight(G * D, fWkT, D, D),
              load_weight(2 * G * D, fWvT, D, D)]
        p.do("gpsimd", lambda E, o=xn[0][:, 0::CH + 1]: E.memset(o, 0.0), None)
        p.do("gpsimd", lambda E, o=xn[1][:, 0::CH + 1]: E.memset(o, 0.0), "sG")
        guard_ready = p.cur("sG")
        for c in range(NCH):
            c0 = c * CH
            p.barrier()
            lv = [dma_load(xch[b][:], xslice(src_dram, b, c0), f"x{b}") for b in range(B)]
            cvals, svals = cast_and_square(xch, lv)
            bval = ln_stats(xbf, xsq, G, cvals, svals)
            if c > 0:
                for b in range(B):
                    p.do("vector", lambda E, o=xn[b][:, 0::CH + 1], a=xn[b][:, CH::CH + 1]:
                         E.tensor_copy(out=o, in_=a), "sV")
            else:
                p.wait("vector", "sG", guard_ready)
            build_xn(xch, bval)
            if level == 1:
                p.barrier()
                for b in range(B):
                    for g in range(G):
                        dma_cast(dst_dram[b, g, :, c0:c0 + CH],
                                 xn[b][:, g * (CH + 1) + 1:(g + 1) * (CH + 1)], "dbg")
                continue
            p.wait("vector", "sT", p.cur("sT"))
            for g in range(G):
                for b in range(B):
                    p.do("vector", lambda E, o=tp[b][:],
                         a=xn[b][:, g * (CH + 1):g * (CH + 1) + CH], m1=vec["mur1"][:, g:g + 1]:
                         E.tensor_scalar(out=o, in0=a, scalar1=m1, scalar2=None, op0=alu.mult), "sV")
                for b in range(B):
                    p.do("vector", lambda E, o=mixr[b][:, g * CH:(g + 1) * CH],
                         a=xn[b][:, g * (CH + 1) + 1:(g + 1) * (CH + 1)],
                         m=vec["mur"][:, g:g + 1], t=tp[b][:]:
                         E.scalar_tensor_tensor(out=o, in0=a, scalar=m, in1=t,
                                                op0=alu.mult, op1=alu.add), "sV")
            for g in range(G):
                for b in range(B):
                    p.do("vector", lambda E, o=tp[b][:],
                         a=xn[b][:, g * (CH + 1):g * (CH + 1) + CH], m1=vec["muk1"][:, g:g + 1]:
                         E.tensor_scalar(out=o, in0=a, scalar1=m1, scalar2=None, op0=alu.mult), "sV")
                for b in range(B):
                    p.do("vector", lambda E, o=mixk[b][:, g * CH:(g + 1) * CH],
                         a=xn[b][:, g * (CH + 1) + 1:(g + 1) * (CH + 1)],
                         m=vec["muk"][:, g:g + 1], t=tp[b][:]:
                         E.scalar_tensor_tensor(out=o, in0=a, scalar=m, in1=t,
                                                op0=alu.mult, op1=alu.add), "sV")
            if level in (2, 3):
                p.barrier()
                srcb = mixr if level == 2 else mixk
                for b in range(B):
                    dma_cast(xslice(dst_dram, b, c0), srcb[b][:], "dbg")
                continue
            mixv = p.cur("sV")
            p.wait("tensor", wv[2])
            p.wait("tensor", "sV", mixv)
            for m in range(G):
                for b in range(B):
                    i, bank = banks.acquire()
                    mm_group(bank, wslices(G * D, m, G, D), rslices(mixk[b], G))
                    banks.wrote(i)
                    banks.pre_read(i, "scalar")
                    tslot = gt[(m % 2) * 2 + b]
                    v = p.do("scalar", lambda E, o=tslot[:], s=bank:
                             E.activation(out=o, in_=s[:], func=AF.Relu, bias=0.0,
                                          scale=1.0), "sA")
                    banks.post_read(i, "sA", v)
                    if level == 4:
                        p.barrier()
                        dma_store(dst_dram[b, m, :, c0:c0 + CH], tslot[:], "dbg")
                        p.barrier()
            if level == 4:
                continue
            for m in range(G):
                for b in range(B):
                    i, bank = banks.acquire()
                    mm_group(bank, wslices(0, m, G, D), rslices(mixr[b], G))
                    banks.wrote(i)
                    banks.pre_read(i, "scalar")
                    gslot = gt[(m % 2) * 2 + b]
                    v = p.do("scalar", lambda E, o=gslot[:], s=bank:
                             E.activation(out=o, in_=s[:], func=AF.Sigmoid, bias=0.0,
                                          scale=1.0), "sA")
                    banks.post_read(i, "sA", v)
                    p.barrier()
                    dma_store(dst_dram[b, m, :, c0:c0 + CH], gslot[:], "dbg")
                    p.barrier()

    phase_list = [
        (phase0, None),
        (lambda: phase_l0ffn(x_in, xresB), xresB),
        (lambda: phase_l0attn_rkv(xresB), xresB),
        (phase_mixer, xresB),
        (lambda: phase_x1(xresB), xresB),
        (phase_w1, xresB),
        (lambda: phase_lif(h1_s, s1_s, GF, 64), xresB),
        (phase_w2, xresB),
        (lambda: phase_lif(h2_s, ffn_s, G, LC), xresB),
        (lambda: phase_ln2_resid(xresB, xresA), xresA),
        (lambda: phase_l0ffn(xresA, xresB), xresB),
        (phase_l1ffn_quant, xresB),
        (lambda: phase_l1ffn(xresB, xresA), xresA),
        (lambda: phase_l1attn_rkv(xresA, 1), xresA),
        (lambda: phase_wkv(outb_s, "wr", "ur", True), xresA),
        (lambda: phase_l1attn_rkv(xresA, 0), xresA),
        (lambda: phase_wkv(y_s, "wf", "uf", False), xresA),
        (lambda: phase_l1attn_out(xresA, xresB), xresB),
        (lambda: phase_l1ffn(xresB, out_d), out_d),
    ]
    n_real = len(phase_list)
    phase_list += [
        (lambda: phase_dbg(x_in, xresA, 1), xresA),
        (lambda: phase_dbg(x_in, xresA, 2), xresA),
        (lambda: phase_dbg(x_in, xresA, 3), xresA),
        (lambda: phase_dbg(x_in, xresA, 4), xresA),
        (lambda: phase_dbg(x_in, xresA, 5), xresA),
    ]
    if _PHASE_SEQ is not None:
        seq = [phase_list[i] for i in _PHASE_SEQ]
    else:
        n = n_real if limit_phases is None else min(limit_phases, n_real)
        seq = phase_list[:n]
    cur_src = x_in
    for fn, dst in seq:
        p.barrier()
        fn()
        if dst is not None:
            cur_src = dst
    p.barrier()

    if cur_src is not out_d:
        stok = None
        for b in range(B):
            for c in range(NCH):
                c0 = c * CH
                if stok:
                    p.wait("sync", stok)
                lv = dma_load(xch[0][:], xslice(cur_src, b, c0), "x0")
                p.wait("sync", lv)
                stok = dma_store(xslice(out_d, b, c0), xch[0][:], "o")
        p.wait("sync", stok)

    ttok = None
    for nm in _DEBUG_TAPS:
        t = dram[nm]
        dbg = nc.dram_tensor("dbg_" + nm, list(t.shape), t.dtype, kind="ExternalOutput").ap()
        if ttok:
            p.wait("sync", ttok)
        ttok = dma_store(dbg, t[:], "o")
    if ttok:
        p.wait("sync", ttok)

    with nc.Block() as blk:
        @blk.sync
        def _(E):
            for f in p.q["sync"]:
                f(E)

        @blk.tensor
        def _(E):
            for f in p.q["tensor"]:
                f(E)

        @blk.vector
        def _(E):
            for f in p.q["vector"]:
                f(E)

        @blk.scalar
        def _(E):
            for f in p.q["scalar"]:
                f(E)

        @blk.gpsimd
        def _(E):
            for f in p.q["gpsimd"]:
                f(E)

    es.close()
    return nc


def _fm(x):
    b, t, d = x.shape
    return np.ascontiguousarray(x.reshape(b, t, d // 128, 128).transpose(0, 2, 3, 1))


def kernel(**inputs):
    ins = {k: np.asarray(v) for k, v in inputs.items()}
    x = _fm(np.asarray(ins["L0_attn_x"], dtype=np.float32))

    def t2(a):
        return np.ascontiguousarray(np.asarray(a, dtype=np.float32).T)

    def asrows(vec1d, g):
        return np.ascontiguousarray(np.asarray(vec1d, dtype=np.float32).reshape(g, 128).T)

    shared = {
        "l0aWrT": t2(ins["L0_attn_Wr"]), "l0aWkT": t2(ins["L0_attn_Wk"]),
        "l0aWvT": t2(ins["L0_attn_Wv"]), "l0aWoT": t2(ins["L0_attn_Wo"]),
        "l0aW1T": t2(ins["L0_attn_W1"]), "l0aW2T": t2(ins["L0_attn_W2"]),
        "l0ab1": np.asarray(ins["L0_attn_b1"], dtype=np.float32).reshape(1, -1),
        "l0ab2": np.asarray(ins["L0_attn_b2"], dtype=np.float32).reshape(1, -1),
        "l0aln1g": asrows(ins["L0_attn_ln1_g"], G), "l0aln1b": asrows(ins["L0_attn_ln1_b"], G),
        "l0aln2g": asrows(ins["L0_attn_ln2_g"], G), "l0aln2b": asrows(ins["L0_attn_ln2_b"], G),
        "l0fWrT": t2(ins["L0_ffn_W_r"]), "l0fWkT": t2(ins["L0_ffn_W_k"]),
        "l0fWvT": t2(ins["L0_ffn_W_v"]),
        "l0fmur": asrows(ins["L0_ffn_mu_r"], G), "l0fmuk": asrows(ins["L0_ffn_mu_k"], G),
        "l1arkvT": t2(ins["L1_attn_rkv_w"]), "l1aoutwT": t2(ins["L1_attn_out_w"]),
        "l1atd": asrows(ins["L1_attn_time_decay"], G),
        "l1atf": asrows(ins["L1_attn_time_first"], G),
        "l1atdr": asrows(ins["L1_attn_time_decay_rev"], G),
        "l1atfr": asrows(ins["L1_attn_time_first_rev"], G),
        "l1fWfcT": t2(ins["L1_ffn_W_fc"]), "l1fWprojT": t2(ins["L1_ffn_W_proj"]),
        "l1flnw": asrows(ins["L1_ffn_ln_w"], GF), "l1flnb": asrows(ins["L1_ffn_ln_b"], GF),
    }

    nc = build_program(_LIMIT_PHASES)
    in_maps = []
    for core in range(NCORES):
        m = dict(shared)
        m["x"] = np.ascontiguousarray(x[core * B:(core + 1) * B])
        in_maps.append(m)
    res = run_bass_kernel_spmd(nc, in_maps, core_ids=list(range(NCORES)))
    global _LAST_RESULTS
    _LAST_RESULTS = res.results
    parts = [res.results[i]["out"] for i in range(NCORES)]
    full = np.concatenate(parts, axis=0)
    return np.ascontiguousarray(full.transpose(0, 3, 1, 2).reshape(16, T, D)).astype(np.float32)
